# version 07
# baseline (speedup 1.0000x reference)
import math
import numpy as np
import ml_dtypes
from contextlib import ExitStack

import concourse.bass as bass
import concourse.mybir as mybir
from concourse.bass_utils import run_bass_kernel_spmd

F32 = mybir.dt.float32
BF16 = mybir.dt.bfloat16
F16 = mybir.dt.float16
I32 = mybir.dt.int32
AF = mybir.ActivationFunctionType
ALU = mybir.AluOpType
AX = mybir.AxisListType

B, T, E, H, HD = 2, 4096, 1024, 16, 64
NPAIR = 8
SP, NSP = 256, 16
SP2, NSP2 = 512, 8
NG, CPG = 8, 4
NTT = 32
TS, NTS = 512, 8
EPS = 1e-5

def _trunc12(v):
    i = np.array(np.float32(v), np.float32).view(np.uint32)
    i = np.uint32(int(i) & ~0xFFF)
    return float(np.array(i, np.uint32).view(np.float32))
TWO_PI = 2.0 * math.pi
CW1 = _trunc12(TWO_PI)
CW2 = _trunc12(TWO_PI - CW1)
CW3 = float(np.float32(TWO_PI - CW1 - CW2))
INV2PI = float(np.float32(1.0 / TWO_PI))

ENGS = ("tensor", "vector", "scalar", "gpsimd", "sync")
ARENA_N = 26000


def rep(ap, n, pos=1):
    a = [list(d) for d in ap.ap]
    a.insert(pos, [0, n])
    return bass.AP(ap.tensor, ap.offset, a)


class Carver:
    def __init__(self, ar, off=0):
        self.ar = ar
        self.off = off

    def f32(self, parts, n):
        ap = self.ar[0:parts, self.off:self.off + n]
        self.off += n
        assert self.off <= ARENA_N, f"arena overflow {self.off}"
        return ap

    def b16(self, parts, n):
        assert n % 2 == 0
        ap = self.ar[0:parts, self.off:self.off + n // 2].bitcast(BF16)
        self.off += n // 2
        assert self.off <= ARENA_N, f"arena overflow {self.off}"
        return ap

    def f16(self, parts, n):
        assert n % 2 == 0
        ap = self.ar[0:parts, self.off:self.off + n // 2].bitcast(F16)
        self.off += n // 2
        return ap


class Sem:
    def __init__(self, h):
        self.h = h
        self.n = 0


def build(nb=B, npair=NPAIR, taps=(), stage=3):
    nc = bass.Bass("TRN2", target_bir_lowering=False)

    xin = nc.dram_tensor("x", [nb, T, E], F32, kind="ExternalInput")
    wT = {nm: nc.dram_tensor(nm + "T", [E, E], F32, kind="ExternalInput")
          for nm in ("wq", "wk", "wv", "wg", "wo")}
    w1T = nc.dram_tensor("w1T", [E, 4096], F32, kind="ExternalInput")
    w2T = nc.dram_tensor("w2T", [4096, E], F32, kind="ExternalInput")
    wdT = nc.dram_tensor("wdT", [E, 16], F32, kind="ExternalInput")
    rwT = nc.dram_tensor("rwT", [E, 16], F32, kind="ExternalInput")
    vecs = {}
    for nm, n in (("bq", E), ("bk", E), ("bv", E), ("bg", E), ("bo", E), ("bd", 16),
                  ("b1", 4096), ("b2", E), ("rb", 16), ("gamma", E), ("beta", E)):
        vecs[nm] = nc.dram_tensor(nm, [1, n], F32, kind="ExternalInput")
    out = nc.dram_tensor("out", [nb, T, E], F32, kind="ExternalOutput")
    attnspill = nc.dram_tensor("attnspill", [nb, NPAIR, 128, T], BF16)
    gatespill = nc.dram_tensor("gatespill", [nb, NPAIR, 128, T], BF16)
    gonspill = nc.dram_tensor("gonspill", [nb, NPAIR, 128, T], BF16)
    tapt = {}
    for tname, shape, dt in taps:
        tapt[tname] = nc.dram_tensor("tap_" + tname, shape, dt, kind="ExternalOutput")
    tapnames = set(t[0] for t in taps)

    prog = {e: [] for e in ENGS}

    def Eo(eng, fn, *incs):
        for s, k in incs:
            s.n += k
        prog[eng].append((fn, tuple((s.h, k) for s, k in incs)))

    def Wt(eng, sem, val=None):
        v = sem.n if val is None else val
        if v > 0:
            prog[eng].append((lambda e, _h=sem.h, _v=v: e.wait_ge(_h, _v), ()))

    with ExitStack() as st:
        def sbt(name, shape, dt):
            return st.enter_context(nc.sbuf_tensor(name, shape, dt))

        ident = sbt("ident", [128, 128], BF16)
        identf = sbt("identf", [128, 128], F32)
        onesf = sbt("onesf", [1, 128], F32)
        onesb = sbt("onesb", [65, 128], BF16)
        onescol = sbt("onescol", [128, 1], F32)
        consts = sbt("consts", [128, 8], F32)
        rwTf = sbt("rwTf", [128, 8, 16], F32)
        wdTf = sbt("wdTf", [128, 8, 16], F32)
        b1cols = sbt("b1cols", [128, 32], F32)
        bgcols = sbt("bgcols", [128, 8], F32)
        bdrow = sbt("bdrow", [1, 16], F32)
        rbrow = sbt("rbrow", [1, 16], F32)
        kmod4 = sbt("kmod4", [128, 2], F32)
        giota = sbt("giota", [128, 32], F32)
        siota = sbt("siota", [128, 8], F32)
        fiota = sbt("fiota", [1, 512], F32)
        trowb = sbt("trowb", [1, 128], F32)
        invfemb = sbt("invfemb", [1, 64], F32)
        DpatT = sbt("DpatT", [128, 4, 512], F16)
        brows = {nm: sbt(nm + "r", [1, E], BF16) for nm in ("bq", "bk", "bv")}
        xnT = sbt("xnT", [128, 8, T], BF16)
        maskT = sbt("maskT", [16, T], BF16)
        stats = sbt("stats", [128, 4, NTT], F32)
        pdacc = sbt("pdacc", [128, 8, NTT], F32)
        pdmean = sbt("pdmean", [128, 8], F32)
        ldb = sbt("ldb", [128, 16], F32)
        ldrow = sbt("ldrow", [1, 16], F32)
        negld = sbt("negld", [1, 16], F32)
        pdrow = sbt("pdrow", [1, 16], F32)
        rowtmp = sbt("rowtmp", [1, 16], F32)
        statsrow = sbt("statsrow", [1, 64], F32)
        lgbuf = sbt("lgbuf", [128, 3, 16], F32)
        m4col = sbt("m4col", [128, 2], F32)
        maskb = sbt("maskb", [128, 16], BF16)
        xtile = sbt("xtile", [128, 2, E], F32)
        spc = sbt("spc", [128, 8], F32)
        dbglg = sbt("dbglg", [128, 512], F32)
        AR = sbt("AR", [128, ARENA_N], F32)

        psF = st.enter_context(nc.psum_tensor("psbig", [128, 4096], F32))
        psH = psF[:, :].bitcast(BF16)

        s_gp = Sem(st.enter_context(nc.semaphore("s_gp")))
        s_sy = Sem(st.enter_context(nc.semaphore("s_sy")))
        s_pe = Sem(st.enter_context(nc.semaphore("s_pe")))
        s_dv = Sem(st.enter_context(nc.semaphore("s_dv")))
        s_ac = Sem(st.enter_context(nc.semaphore("s_ac")))
        s_gc = Sem(st.enter_context(nc.semaphore("s_gc")))
        ALLS = (s_gp, s_sy, s_pe, s_dv, s_ac, s_gc)

        def barrier(engs=ENGS):
            for e in engs:
                for s in ALLS:
                    Wt(e, s)

        def spacer():
            Eo("vector", lambda e: e.tensor_copy(spc[:, 0:4], consts[:, 0:4]))

        def P0():
            g = "gpsimd"
            Eo(g, lambda e: e.memset(ident[:, :], 1.0))
            Eo(g, lambda e: e.affine_select(ident[:, :], ident[:, :], pattern=[[-1, 128]],
                                            compare_op=ALU.is_equal, fill=0.0, base=0, channel_multiplier=1))
            Eo(g, lambda e: e.memset(identf[:, :], 1.0))
            Eo(g, lambda e: e.affine_select(identf[:, :], identf[:, :], pattern=[[-1, 128]],
                                            compare_op=ALU.is_equal, fill=0.0, base=0, channel_multiplier=1))
            Eo(g, lambda e: e.memset(onesf[:, :], 1.0))
            Eo(g, lambda e: e.memset(onesb[:, :], 1.0))
            Eo(g, lambda e: e.memset(onescol[:, :], 1.0))
            Eo(g, lambda e: e.memset(consts[:, :], 0.0))
            Eo(g, lambda e: e.memset(consts[:, 0:1], float(EPS)))
            Eo(g, lambda e: e.memset(consts[:, 1:2], float(-math.log(8.0) + math.log(1.02))))
            Eo(g, lambda e: e.memset(consts[:, 2:3], float(-5.0 * math.log(2.0))))
            Eo(g, lambda e: e.iota(kmod4[:, :], pattern=[[128, 2]], base=0,
                                   channel_multiplier=1, allow_small_or_imprecise_dtypes=True))
            Eo(g, lambda e: e.iota(giota[:, :], pattern=[[128, 32]], base=0,
                                   channel_multiplier=0, allow_small_or_imprecise_dtypes=True))
            Eo(g, lambda e: e.iota(siota[:, :], pattern=[[512, 8]], base=0,
                                   channel_multiplier=0, allow_small_or_imprecise_dtypes=True))
            Eo(g, lambda e: e.iota(fiota[:, :], pattern=[[1, 512]], base=0,
                                   channel_multiplier=0, allow_small_or_imprecise_dtypes=True))
            Eo(g, lambda e: e.iota(
                DpatT[:, :, :],
                pattern=[[128, 4], [-1, 512]], base=0,
                channel_multiplier=1, allow_small_or_imprecise_dtypes=True))
            Eo(g, lambda e: e.iota(invfemb[:, :].rearrange("p (a b) -> p a b", a=2),
                                   pattern=[[0, 2], [1, 32]], base=0,
                                   channel_multiplier=0, allow_small_or_imprecise_dtypes=True), (s_gc, 1))
            Eo(g, lambda e: e.dma_start(rwTf[:, :, :], rwT[:, :].rearrange("(c p) o -> p c o", p=128)), (s_gp, 16))
            Eo(g, lambda e: e.dma_start(wdTf[:, :, :], wdT[:, :].rearrange("(c p) o -> p c o", p=128)), (s_gp, 16))
            for c in range(32):
                Eo(g, lambda e, c=c: e.dma_start(b1cols[:, c:c + 1],
                    vecs["b1"][0:1, 128 * c:128 * c + 128].rearrange("q (p a) -> (q p) a", a=1)), (s_gp, 16))
            for c in range(8):
                Eo(g, lambda e, c=c: e.dma_start(bgcols[:, c:c + 1],
                    vecs["bg"][0:1, 128 * c:128 * c + 128].rearrange("q (p a) -> (q p) a", a=1)), (s_gp, 16))
            Eo(g, lambda e: e.dma_start(bdrow[:, :], vecs["bd"][:, :]), (s_gp, 16))
            Eo(g, lambda e: e.dma_start(rbrow[:, :], vecs["rb"][:, :]), (s_gp, 16))
            for nm in ("bq", "bk", "bv"):
                Eo(g, lambda e, nm=nm: e.dma_start(brows[nm][:, :], vecs[nm][:, :]), (s_gp, 16))
            Wt("vector", s_gc)
            Eo("vector", lambda e: e.tensor_scalar(
                DpatT[:, :, :], DpatT[:, :, :], 0.0, None, ALU.max), (s_dv, 1))
            Wt("scalar", s_gc)
            Eo("scalar", lambda e: e.activation(invfemb[:, :], invfemb[:, :], AF.Exp,
                                                scale=float(-math.log(10000.0) / 32.0)), (s_ac, 1))

        def trig(costab, sinsig, tsc):
            for tt in range(NTT):
                Wt("gpsimd", s_pe, s_pe.n)
                Eo("gpsimd", lambda e, tt=tt: e.iota(trowb[:, :], pattern=[[1, 128]], base=128 * tt,
                                                     channel_multiplier=0, allow_small_or_imprecise_dtypes=True),
                   (s_gc, 1))
                slot = tt % 2
                ps_th = psF[:, 3072 + 512 * slot:3072 + 512 * slot + 64]
                Wt("tensor", s_gc)
                Wt("tensor", s_ac)
                Eo("tensor", lambda e, o=ps_th: e.matmul(o, trowb[:, :], invfemb[:, :],
                                                         start=True, stop=True), (s_pe, 1))
                Wt("vector", s_pe)
                for wh in range(2):
                    off = 0.0 if wh == 0 else math.pi / 2.0
                    t0 = tsc[:, 256 * wh:256 * wh + 64]
                    t1 = tsc[:, 256 * wh + 64:256 * wh + 128]
                    t2 = tsc[:, 256 * wh + 128:256 * wh + 192]
                    t3 = tsc[:, 256 * wh + 192:256 * wh + 256]
                    Eo("vector", lambda e, o=ps_th, off=off, t0=t0: e.tensor_scalar(
                        t0, o, float(INV2PI), float(off * INV2PI), ALU.mult, ALU.add))
                    spacer()
                    Eo("vector", lambda e, t0=t0, t1=t1: e.tensor_copy(t1.bitcast(I32)[:, 0:64], t0))
                    spacer()
                    Eo("vector", lambda e, t1=t1, t2=t2: e.tensor_copy(t2, t1.bitcast(I32)[:, 0:64]))
                    spacer()
                    Eo("vector", lambda e, o=ps_th, t2=t2, t0=t0: e.scalar_tensor_tensor(
                        t0, t2, float(-CW1), o, ALU.mult, ALU.add))
                    spacer()
                    if off:
                        Eo("vector", lambda e, off=off, t0=t0: e.tensor_scalar(
                            t0, t0, float(off), None, ALU.add))
                        spacer()
                    Eo("vector", lambda e, t2=t2, t0=t0, t3=t3: e.scalar_tensor_tensor(
                        t3, t2, float(-CW2), t0, ALU.mult, ALU.add))
                    spacer()
                    Eo("vector", lambda e, t2=t2, t3=t3, t0=t0: e.scalar_tensor_tensor(
                        t0, t2, float(-CW3), t3, ALU.mult, ALU.add), (s_dv, 1))
                    Wt("scalar", s_dv)
                    dst = sinsig if wh == 0 else costab
                    Eo("scalar", lambda e, tt=tt, dst=dst, t0=t0: e.activation(
                        dst[:, tt, :], t0, AF.Sin), (s_ac, 1))
                Wt("vector", s_ac)
                ev = sinsig[:, tt, :].rearrange("p (a b) -> p a b", b=2)[:, :, 0:1]
                Eo("vector", lambda e, ev=ev: e.tensor_scalar(ev, ev, -1.0, None, ALU.mult), (s_dv, 1))

        def P1(b, cv):
            xnf = cv.f32(128, 1024)
            xnb = cv.b16(128, 1024)
            junk = cv.f32(128, 1024)
            xfT = cv.f32(128, 256)
            j2 = cv.f32(128, 128)
            xfree = {0: 0, 1: 0}
            for tt in range(NTT):
                slot = tt % 2
                Wt("sync", s_dv, xfree[slot])
                Eo("sync", lambda e, b=b, tt=tt, slot=slot: e.dma_start(
                    xtile[:, slot, :], xin[b, 128 * tt:128 * tt + 128, :]), (s_sy, 16))
                Wt("vector", s_sy)
                Eo("vector", lambda e, tt=tt, slot=slot, junk=junk: e.tensor_scalar(
                    junk, xtile[:, slot, :], 1.0, 0.0, ALU.mult, ALU.add, accum_out=stats[:, 0, tt:tt + 1]))
                Eo("vector", lambda e, tt=tt, slot=slot, junk=junk: e.scalar_tensor_tensor(
                    junk, xtile[:, slot, :], 1.0, xtile[:, slot, :], ALU.bypass, ALU.mult,
                    accum_out=stats[:, 1, tt:tt + 1]), (s_dv, 1))
                xfree[slot] = s_dv.n
            Eo("vector", lambda e: e.tensor_scalar(stats[:, 2, :], stats[:, 0, :], float(1.0 / E), None, ALU.mult))
            Eo("vector", lambda e, junk=junk: e.tensor_scalar(junk[:, 0:NTT], stats[:, 1, :], float(1.0 / E), None, ALU.mult))
            Eo("vector", lambda e, junk=junk: e.tensor_mul(junk[:, NTT:2 * NTT], stats[:, 2, :], stats[:, 2, :]))
            spacer()
            Eo("vector", lambda e, junk=junk: e.tensor_sub(stats[:, 3, :], junk[:, 0:NTT], junk[:, NTT:2 * NTT]), (s_dv, 1))
            Wt("scalar", s_dv)
            Eo("scalar", lambda e, junk=junk: e.activation(junk[:, 0:NTT], stats[:, 3, :], AF.Ln, bias=consts[:, 0:1]))
            Eo("scalar", lambda e, junk=junk: e.activation(stats[:, 3, :], junk[:, 0:NTT], AF.Exp, scale=-0.5), (s_ac, 1))
            lgT = psF[:, 1024:1152]
            lgtm = psF[:, 1536:1552]
            mtp = psH[:, 4096:4224]
            xfree = {0: s_dv.n, 1: s_dv.n}
            for tt in range(NTT):
                slot = tt % 2
                Wt("sync", s_dv, xfree[slot])
                Eo("sync", lambda e, b=b, tt=tt, slot=slot: e.dma_start(
                    xtile[:, slot, :], xin[b, 128 * tt:128 * tt + 128, :]), (s_sy, 16))
                Wt("vector", s_sy)
                Wt("vector", s_ac)
                Eo("vector", lambda e, tt=tt, slot=slot, xnf=xnf: e.tensor_scalar(
                    xnf, xtile[:, slot, :], stats[:, 2, tt:tt + 1], stats[:, 3, tt:tt + 1],
                    ALU.subtract, ALU.mult), (s_dv, 1))
                xfree[slot] = s_dv.n
                spacer()
                Eo("vector", lambda e, xnf=xnf, xnb=xnb: e.tensor_copy(xnb, xnf), (s_dv, 1))
                for c in range(8):
                    pstr = psH[:, 1024 * (c % 2):1024 * (c % 2) + 128]
                    psxf = psF[:, 3072 + 512 * (c % 2):3072 + 512 * (c % 2) + 128]
                    Wt("tensor", s_dv)
                    Eo("tensor", lambda e, c=c, o=pstr, xnb=xnb: e.transpose(
                        o, xnb[:, 128 * c:128 * c + 128], ident[:, :]))
                    Eo("tensor", lambda e, c=c, o=psxf, xnf=xnf: e.transpose(
                        o, xnf[:, 128 * c:128 * c + 128], identf[:, :]), (s_pe, 1))
                    Wt("vector", s_pe)
                    Eo("vector", lambda e, c=c, tt=tt, i=pstr: e.tensor_scalar(
                        xnT[:, c, 128 * tt:128 * tt + 128], i, 1.0, 0.0, ALU.mult, ALU.add,
                        accum_out=pdacc[:, c, tt:tt + 1]))
                    Eo("vector", lambda e, c=c, i=psxf, xfT=xfT: e.tensor_copy(
                        xfT[:, 128 * (c % 2):128 * (c % 2) + 128], i), (s_dv, 1))
                    Wt("tensor", s_dv)
                    Eo("tensor", lambda e, c=c, o=lgT, xfT=xfT: e.matmul(
                        o[0:16, :], rwTf[:, c, :], xfT[:, 128 * (c % 2):128 * (c % 2) + 128],
                        start=(c == 0), stop=False))
                Eo("tensor", lambda e, o=lgT: e.matmul(o[0:16, :], rbrow[:, :], onesf[:, :],
                                                       start=False, stop=True), (s_pe, 1))
                Wt("vector", s_pe)
                Eo("vector", lambda e, o=lgT, j2=j2: e.tensor_copy(j2[0:16, 0:128], o[0:16, :]), (s_dv, 1))
                Wt("tensor", s_dv)
                Eo("tensor", lambda e, o=lgtm, j2=j2: e.transpose(o, j2[0:16, 0:128], identf[0:16, 0:16]), (s_pe, 1))
                Wt("vector", s_pe)
                Eo("vector", lambda e, i=lgtm: e.tensor_copy(lgbuf[:, 0, :], i))
                Eo("vector", lambda e, i=lgtm: e.tensor_copy(lgbuf[:, 1, :], i), (s_dv, 1))
                Eo("vector", lambda e, tt=tt, i=lgtm: e.tensor_copy(dbglg[:, 16 * tt:16 * tt + 16], i))
                for it in range(3):
                    spacer()
                    Eo("vector", lambda e: e.reduce_max(m4col[:, 0:1], lgbuf[:, 1, :], AX.X))
                    spacer()
                    Eo("vector", lambda e: e.tensor_single_scalar(lgbuf[:, 2, :], lgbuf[:, 1, :], m4col[:, 0:1], ALU.is_ge))
                    spacer()
                    Eo("vector", lambda e: e.scalar_tensor_tensor(
                        lgbuf[:, 1, :], lgbuf[:, 2, :], -1e30, lgbuf[:, 1, :], ALU.mult, ALU.add))
                spacer()
                Eo("vector", lambda e: e.reduce_max(m4col[:, 1:2], lgbuf[:, 1, :], AX.X))
                spacer()
                Eo("vector", lambda e: e.tensor_single_scalar(maskb[:, :], lgbuf[:, 0, :], m4col[:, 1:2], ALU.is_ge),
                   (s_dv, 1))
                Wt("tensor", s_dv)
                Eo("tensor", lambda e, o=mtp: e.transpose(o[0:16, :], maskb[:, :], ident[:, :]), (s_pe, 1))
                Wt("vector", s_pe)
                Eo("vector", lambda e, tt=tt, i=mtp: e.tensor_copy(
                    maskT[:, 128 * tt:128 * tt + 128], i[0:16, :]), (s_dv, 1))
            Eo("vector", lambda e: e.tensor_reduce(pdmean[:, :], pdacc[:, :, :], AX.X, ALU.add))
            spacer()
            Eo("vector", lambda e: e.tensor_scalar(pdmean[:, :], pdmean[:, :], float(1.0 / T), None, ALU.mult), (s_dv, 1))
            pdp = psF[:, 1664:1680]
            Wt("tensor", s_dv)
            for c in range(8):
                Eo("tensor", lambda e, c=c, o=pdp: e.matmul(o[0:1, :], pdmean[:, c:c + 1], wdTf[:, c, :],
                                                            start=(c == 0), stop=False))
            Eo("tensor", lambda e, o=pdp: e.matmul(o[0:1, :], onesf[:, 0:1], bdrow[:, :],
                                                   start=False, stop=True), (s_pe, 1))
            Wt("scalar", s_pe)
            Eo("scalar", lambda e, i=pdp: e.activation(rowtmp[:, :], i[0:1, :], AF.Tanh, scale=0.5), (s_ac, 1))
            Wt("vector", s_ac)
            Eo("vector", lambda e: e.tensor_scalar(pdrow[:, :], rowtmp[:, :], 0.5, 0.5, ALU.mult, ALU.add), (s_dv, 1))
            Wt("scalar", s_dv)
            Eo("scalar", lambda e: e.activation(rowtmp[:, :], pdrow[:, :], AF.Exp,
                                                scale=float(-8.0 * math.log(2.0)), bias=consts[0:1, 2:3]), (s_ac, 1))
            Wt("vector", s_ac)
            Eo("vector", lambda e: e.tensor_scalar(ldrow[:, :], rowtmp[:, :], -1.0, 1.0, ALU.mult, ALU.add), (s_dv, 1))
            Wt("scalar", s_dv)
            Eo("scalar", lambda e: e.activation(ldrow[:, :], ldrow[:, :], AF.Ln), (s_ac, 1))
            Wt("vector", s_ac)
            Eo("vector", lambda e: e.tensor_scalar(negld[:, :], ldrow[:, :], -1.0, None, ALU.mult), (s_dv, 1))
            ldp = psF[:, 1024:1040]
            Wt("tensor", s_dv)
            Eo("tensor", lambda e, o=ldp: e.matmul(o, onesf[:, :], ldrow[:, :], start=True, stop=True), (s_pe, 1))
            Wt("vector", s_pe)
            Eo("vector", lambda e, i=ldp: e.tensor_copy(ldb[:, :], i), (s_dv, 1))
            if "xnt" in tapnames:
                barrier(("sync",))
                Eo("sync", lambda e: e.dma_start(tapt["xnt"][:, :, :], xnT[:, :, :]), (s_sy, 16))
                Eo("sync", lambda e: e.dma_start(tapt["mask"][:, :], maskT[:, :]), (s_sy, 16))
                Eo("sync", lambda e: e.dma_start(tapt["ld"][:, 0:16], ldrow[:, :]), (s_sy, 16))
                Eo("sync", lambda e: e.dma_start(tapt["ld"][:, 16:32], pdrow[:, :]), (s_sy, 16))
                if "lgt" in tapnames:
                    Eo("sync", lambda e: e.dma_start(tapt["lgt"][:, :], dbglg[:, :]), (s_sy, 16))

        def P2(b, p, cv, costab, sinsig):
            QT = cv.b16(128, T)
            KT = cv.b16(128, T)
            Vb = cv.b16(128, NTT * 130)
            V2b = cv.b16(128, NTT * 130)
            Pbuf = cv.b16(128, 2 * 1024)
            undA = cv.b16(65, T)
            undB = cv.b16(65, T)
            expD = cv.b16(128, 4 * 1024)
            wq_b = cv.b16(128, 8 * 128)
            wk_b = cv.b16(128, 8 * 128)
            wv_b = cv.b16(128, 8 * 128)
            wg_b = cv.b16(128, 8 * 128)
            grawS = cv.b16(128, 1024)
            ropef = cv.f32(128, 128)
            ropeg = cv.f32(128, 128)
            ropeb = cv.b16(128, 2 * 128)
            qn2 = cv.f32(128, 128)
            biasm = cv.f32(128, 2 * 256)
            ffac = cv.f32(128, 8)
            gbufq = cv.f32(65, 2 * 512)
            crow = cv.f32(1, 8)
            negcB = cv.f32(128, 1)
            jM = cv.f32(128, 128)
            s1b = cv.f32(64, 32)
            s2b = cv.f32(64, 2)
            s12 = cv.f32(64, 4)
            jn = cv.f32(128, 1024)
            wmap = {"wq": wq_b, "wk": wk_b, "wv": wv_b, "wg": wg_b}

            barrier()
            for nm in ("wq", "wk", "wv", "wg"):
                Eo("gpsimd", lambda e, nm=nm, p=p, w=wmap[nm]: e.dma_start(
                    w.rearrange("p (c o) -> p c o", c=8),
                    wT[nm][:, 128 * p:128 * p + 128].rearrange("(c q) o -> q c o", q=128)), (s_gp, 16))
            Eo("gpsimd", lambda e, Vb=Vb: e.memset(Vb, 1.0), (s_gc, 1))
            Wt("tensor", s_gp)
            Wt("tensor", s_gc)
            pjfree = {0: 0, 1: 0}
            for tens, wb, brn in (("q", wq_b, "bq"), ("k", wk_b, "bk"), ("v", wv_b, "bv")):
                pend = None
                for tt in range(NTT + 1):
                    if tt < NTT:
                        pj = psF[:, 3072 + 512 * (tt % 2):3072 + 512 * (tt % 2) + 128]
                        Wt("tensor", s_dv, pjfree[tt % 2])
                        for c in range(8):
                            Eo("tensor", lambda e, c=c, tt=tt, wb=wb, o=pj: e.matmul(
                                o, xnT[:, c, 128 * tt:128 * tt + 128],
                                wb.rearrange("p (c o) -> p c o", c=8)[:, c, :],
                                start=(c == 0), stop=False))
                        Eo("tensor", lambda e, o=pj, brn=brn, p=p: e.matmul(
                            o, onesb[0:1, :], brows[brn][:, 128 * p:128 * p + 128],
                            start=False, stop=True), (s_pe, 1))
                        Wt("vector", s_pe)
                        if tens == "v":
                            Eo("vector", lambda e, tt=tt, i=pj, Vb=Vb: e.tensor_scalar(
                                Vb[:, 130 * tt:130 * tt + 64], i[:, 0:64], 0.0, None, ALU.add))
                            Eo("vector", lambda e, tt=tt, i=pj, Vb=Vb: e.tensor_scalar(
                                Vb[:, 130 * tt + 65:130 * tt + 129], i[:, 64:128], 0.0, None, ALU.add), (s_dv, 1))
                            pjfree[tt % 2] = s_dv.n
                        else:
                            pj4 = pj.rearrange("p (h a b) -> p h a b", h=2, b=2)
                            rf4 = ropef.rearrange("p (h a b) -> p h a b", h=2, b=2)
                            rg4 = ropeg.rearrange("p (h a b) -> p h a b", h=2, b=2)
                            cos4 = rep(costab[:, tt, :].rearrange("p (a b) -> p a b", b=2), 2)
                            sin_e = rep(sinsig[:, tt, :].rearrange("p (a b) -> p a b", b=2)[:, :, 0:1], 2)
                            sin_o = rep(sinsig[:, tt, :].rearrange("p (a b) -> p a b", b=2)[:, :, 1:2], 2)
                            Eo("vector", lambda e, pj4=pj4, cos4=cos4, rf4=rf4: e.tensor_tensor(
                                rf4, pj4, cos4, ALU.mult))
                            Eo("vector", lambda e, pj4=pj4, sin_e=sin_e, rg4=rg4: e.tensor_tensor(
                                rg4[:, :, :, 0:1], pj4[:, :, :, 1:2], sin_e, ALU.mult))
                            Eo("vector", lambda e, pj4=pj4, sin_o=sin_o, rg4=rg4: e.tensor_tensor(
                                rg4[:, :, :, 1:2], pj4[:, :, :, 0:1], sin_o, ALU.mult), (s_dv, 1))
                            pjfree[tt % 2] = s_dv.n
                            rsl = tt % 2
                            rb2 = ropeb.rearrange("p (s n) -> p s n", s=2)
                            spacer()
                            Eo("vector", lambda e, rsl=rsl, rb2=rb2, ropef=ropef, ropeg=ropeg: e.tensor_add(
                                rb2[:, rsl, :], ropef, ropeg), (s_dv, 1))
                            spacer()
                            qk = 0 if tens == "q" else 1
                            for h in range(2):
                                Eo("vector", lambda e, rsl=rsl, h=h, qk=qk, tt=tt, rb2=rb2, jM=jM, qn2=qn2: e.scalar_tensor_tensor(
                                    jM[:, 0:64], rb2[:, rsl, 64 * h:64 * h + 64], 1.0,
                                    rb2[:, rsl, 64 * h:64 * h + 64], ALU.bypass, ALU.mult,
                                    accum_out=qn2[:, 64 * qk + 32 * h + tt:64 * qk + 32 * h + tt + 1]), (s_dv, 1))
                    if tens != "v" and pend is not None:
                        ptt, prsl, pdv = pend
                        ptr = psH[:, 1024 * (ptt % 2):1024 * (ptt % 2) + 128]
                        rb2b = ropeb.rearrange("p (s n) -> p s n", s=2)
                        Wt("tensor", s_dv, pdv)
                        Eo("tensor", lambda e, prsl=prsl, o=ptr, rb2b=rb2b: e.transpose(
                            o, rb2b[:, prsl, :], ident[:, :]), (s_pe, 1))
                        Wt("vector", s_pe)
                        dst = QT if tens == "q" else KT
                        Eo("vector", lambda e, ptt=ptt, i=ptr, dst=dst: e.tensor_copy(
                            dst[:, 128 * ptt:128 * ptt + 128], i), (s_dv, 1))
                    if tt < NTT and tens != "v":
                        pend = (tt, tt % 2, s_dv.n)
                    else:
                        pend = pend if tt < NTT else None
            gspf = {0: 0, 1: 0}
            for sp_ in range(8):
                gps = psF[:, 2048 + 512 * (sp_ % 2):2048 + 512 * (sp_ % 2) + 512]
                Wt("tensor", s_dv, s_dv.n)
                for c in range(8):
                    Eo("tensor", lambda e, c=c, sp_=sp_, o=gps, wg_b=wg_b: e.matmul(
                        o, wg_b.rearrange("p (c o) -> p c o", c=8)[:, c, :],
                        xnT[:, c, 512 * sp_:512 * sp_ + 512], start=(c == 0), stop=(c == 7)),
                       *([(s_pe, 1)] if c == 7 else []))
                Wt("vector", s_pe)
                Wt("vector", s_gp, gspf[sp_ % 2])
                Eo("vector", lambda e, sp_=sp_, i=gps, p=p, grawS=grawS: e.tensor_scalar(
                    grawS.rearrange("p (s n) -> p s n", s=2)[:, sp_ % 2, :], i, bgcols[:, p:p + 1], None, ALU.add),
                   (s_dv, 1))
                Wt("gpsimd", s_dv)
                Eo("gpsimd", lambda e, b=b, p=p, sp_=sp_, grawS=grawS: e.dma_start(
                    gatespill[b, p, :, 512 * sp_:512 * sp_ + 512],
                    grawS.rearrange("p (s n) -> p s n", s=2)[:, sp_ % 2, :]), (s_gp, 16))
                gspf[sp_ % 2] = s_gp.n
            if stage < 2:
                return
            for h in range(2):
                Eo("scalar", lambda e, h=h, p=p, ffac=ffac: e.activation(
                    ffac.rearrange("p (h a) -> p h a", h=2)[:, h, 0:1], kmod4[:, 0:1], AF.Exp,
                    scale=ldb[:, 2 * p + h:2 * p + h + 1]), (s_ac, 1))
            Wt("vector", s_ac)
            for tt in range(NTT):
                for h in range(2):
                    Eo("vector", lambda e, tt=tt, h=h, Vb=Vb, V2b=V2b, ffac=ffac: e.tensor_scalar(
                        V2b[:, 130 * tt + 65 * h:130 * tt + 65 * h + 65],
                        Vb[:, 130 * tt + 65 * h:130 * tt + 65 * h + 65],
                        ffac.rearrange("p (h a) -> p h a", h=2)[:, h, 0:1], None, ALU.mult),
                       (s_dv, 1))
            tp1 = psF[:, 1024:1152]
            Wt("tensor", s_dv)
            Eo("tensor", lambda e, o=tp1, qn2=qn2: e.transpose(o, qn2, identf[:, :]), (s_pe, 1))
            Wt("vector", s_pe)
            Eo("vector", lambda e, i=tp1, jM=jM: e.tensor_reduce(jM[:, 0:1], i, AX.X, ALU.max), (s_dv, 1))
            tp2 = psF[:, 1536:1664]
            Wt("tensor", s_dv)
            Eo("tensor", lambda e, o=tp2, jM=jM: e.transpose(o[0:1, :], jM[:, 0:1], identf[:, :]), (s_pe, 1))
            Wt("vector", s_pe)
            Eo("vector", lambda e, i=tp2, crow=crow: e.tensor_reduce(
                crow[:, 0:4], i[0:1, :].rearrange("p (a b) -> p a b", a=4), AX.X, ALU.max), (s_dv, 1))
            spacer()
            Eo("vector", lambda e, crow=crow: e.tensor_reduce(crow[:, 4:5], crow[:, 0:2], AX.X, ALU.max))
            Eo("vector", lambda e, crow=crow: e.tensor_reduce(crow[:, 5:6], crow[:, 2:4], AX.X, ALU.max))
            spacer()
            Eo("vector", lambda e, crow=crow: e.tensor_mul(crow[:, 6:7], crow[:, 4:5], crow[:, 5:6]), (s_dv, 1))
            Wt("scalar", s_dv)
            Eo("scalar", lambda e, crow=crow: e.activation(crow[:, 7:8], crow[:, 6:7], AF.Ln))
            Eo("scalar", lambda e, crow=crow: e.activation(crow[:, 0:1], crow[:, 7:8], AF.Exp,
                                                           scale=0.5, bias=consts[0:1, 1:2]), (s_ac, 1))
            cb = psF[:, 1024:1025]
            Wt("tensor", s_ac)
            Eo("tensor", lambda e, o=cb, crow=crow: e.matmul(o, onesf[:, :], crow[:, 0:1], start=True, stop=True), (s_pe, 1))
            Wt("vector", s_pe)
            Eo("vector", lambda e, i=cb, negcB=negcB: e.tensor_scalar(negcB, i, -1.0, None, ALU.mult), (s_dv, 1))
            for h in range(2):
                Eo("vector", lambda e, h=h, p=p, jM=jM: e.tensor_scalar(
                    jM[:, 64 * h:64 * h + 32], giota[:, :], ldb[:, 2 * p + h:2 * p + h + 1], None, ALU.mult))
            for h in range(2):
                Eo("vector", lambda e, h=h, p=p, jM=jM: e.tensor_scalar(
                    jM[:, 64 * h + 32:64 * h + 40], siota[:, :], ldb[:, 2 * p + h:2 * p + h + 1], None, ALU.mult))
            spacer()
            for h in range(2):
                Eo("vector", lambda e, h=h, jM=jM, negcB=negcB: e.tensor_scalar(
                    jM[:, 64 * h + 40:64 * h + 48], jM[:, 64 * h + 32:64 * h + 40], negcB[:, 0:1], None, ALU.subtract))
            spacer()
            for h in range(2):
                bm = biasm.rearrange("p (h g s) -> p h g s", h=2, g=32)[:, h, :, :]
                m1 = rep(jM[:, 64 * h:64 * h + 32], 8, pos=2)
                m2 = rep(jM[:, 64 * h + 40:64 * h + 48], 32, pos=1)
                Eo("vector", lambda e, bm=bm, m1=m1, m2=m2: e.tensor_tensor(bm, m1, m2, ALU.subtract), (s_dv, 1))
            Wt("scalar", s_dv)
            for h in range(2):
                Eo("scalar", lambda e, h=h, p=p, expD=expD: e.activation(
                    expD.rearrange("p (x a b) -> p x a b", x=2, a=4)[:, h, :, :],
                    DpatT[:, :, :], AF.Exp, scale=ldb[:, 2 * p + h:2 * p + h + 1]), (s_ac, 1))
            for h in range(2):
                Eo("scalar", lambda e, h=h, p=p, jn=jn: e.activation(
                    jn[0:1, 512 * h:512 * h + 512], fiota[:, :], AF.Exp,
                    scale=negld[:, 2 * p + h:2 * p + h + 1]), (s_ac, 1))
            for h in range(2):
                gp_ = psF[:, 1024 + 512 * h:1024 + 512 * h + 512]
                Wt("tensor", s_ac)
                Eo("tensor", lambda e, h=h, o=gp_, jn=jn: e.matmul(
                    o[0:65, :], onesf[:, 0:65], jn[0:1, 512 * h:512 * h + 512],
                    start=True, stop=True), (s_pe, 1))
                Wt("vector", s_pe)
                Eo("vector", lambda e, h=h, i=gp_, gbufq=gbufq: e.tensor_copy(
                    gbufq.rearrange("p (h n) -> p h n", h=2)[:, h, :], i[0:65, :]), (s_dv, 1))
            S_R = [psF[:, 0:1024], psF[:, 1024:2048]]
            accs = [psF[0:65, 2048:3072], psF[0:65, 3072:4096]]
            Pb2 = Pbuf.rearrange("p (s n) -> p s n", s=2)
            bm2 = biasm.rearrange("p (h n) -> p h n", h=2)
            eD2 = expD.rearrange("p (x n) -> p x n", x=2)
            gb2 = gbufq.rearrange("p (h n) -> p h n", h=2)
            fin_dv = s_dv.n
            NGQ = 32
            Wt("tensor", s_dv)
            Wt("tensor", s_ac)

            def emit_scores(s, g):
                r = g % 2
                for h, base in ((0, 0), (1, 64)):
                    Eo("tensor", lambda e, g=g, h=h, base=base, s=s, r=r, QT=QT, KT=KT: e.matmul(
                        S_R[r][:, 512 * h:512 * h + 512],
                        KT[base:base + 64, 128 * g:128 * g + 128],
                        QT[base:base + 64, SP2 * s:SP2 * s + SP2],
                        start=True, stop=True),
                       *([(s_pe, 1)] if h == 1 else []))

            exp_done = {}
            av_done = {}
            for s in range(NSP2):
                Wt("tensor", s_ac, exp_done.get((s - 1, NGQ - 2), 0))
                emit_scores(s, 0)
                for g in range(NGQ):
                    r = g % 2
                    cls = "past" if g < 4 * s else ("diag" if g <= 4 * s + 3 else "fut")
                    sc_cnt = s_pe.n
                    Wt("scalar", s_pe, sc_cnt)
                    Wt("scalar", s_pe, av_done.get(g - 2 if g >= 2 else NGQ + g - 2, 0))
                    if cls != "fut":
                        Eo("scalar", lambda e, r=r, g=g, Pb2=Pb2, negcB=negcB: e.activation(
                            Pb2[:, g % 2, :], S_R[r], AF.Exp, bias=negcB[:, 0:1], scale=0.125), (s_ac, 1))
                    else:
                        for h in range(2):
                            Eo("scalar", lambda e, r=r, g=g, h=h, s=s, Pb2=Pb2, bm2=bm2: e.activation(
                                Pb2[:, g % 2, 512 * h:512 * h + 512], S_R[r][:, 512 * h:512 * h + 512],
                                AF.Exp, bias=bm2[:, h, 8 * g + s:8 * g + s + 1], scale=0.125), (s_ac, 1))
                    exp_done[(s, g)] = s_ac.n
                    if g + 1 < NGQ:
                        Wt("tensor", s_ac, exp_done.get((s, g - 1), exp_done.get((s - 1, NGQ - 1), 0)))
                        emit_scores(s, g + 1)
                    dv_gate = None
                    if cls == "diag":
                        d = g - 4 * s
                        Wt("vector", s_ac, exp_done[(s, g)])
                        for h in range(2):
                            Eo("vector", lambda e, g=g, h=h, d=d, Pb2=Pb2, eD2=eD2: e.tensor_tensor(
                                Pb2[:, g % 2, 512 * h:512 * h + 512], Pb2[:, g % 2, 512 * h:512 * h + 512],
                                eD2[:, h, 512 * d:512 * d + 512], ALU.mult), (s_dv, 1))
                        dv_gate = s_dv.n
                    if dv_gate is not None:
                        Wt("tensor", s_dv, dv_gate)
                    else:
                        Wt("tensor", s_ac, exp_done[(s, g)])
                    if g == 0:
                        Wt("tensor", s_dv, fin_dv)
                    Vsrc = V2b if cls == "fut" else Vb
                    ro = 512 if cls == "fut" else 0
                    if cls == "fut":
                        stf, spf = (g == 4 * s + 4), (g == NGQ - 1)
                    else:
                        stf, spf = (g == 0), (g == 4 * s + 3)
                    for h in range(2):
                        acc = accs[h]
                        Eo("tensor", lambda e, g=g, h=h, acc=acc, ro=ro, Vsrc=Vsrc, s=s, stf=stf, spf=spf, Pb2=Pb2: e.matmul(
                            acc[:, ro:ro + 512],
                            Vsrc[:, 130 * g + 65 * h:130 * g + 65 * h + 65],
                            Pb2[:, g % 2, 512 * h:512 * h + 512],
                            start=stf, stop=spf),
                           *([(s_pe, 1)] if h == 1 else []))
                    av_done[g] = s_pe.n
                Wt("vector", s_pe)
                for h in range(2):
                    acc = accs[h]
                    und = undA if h == 0 else undB
                    if s <= NSP2 - 2:
                        Eo("vector", lambda e, acc=acc, h=h, gb2=gb2, jn=jn: e.tensor_tensor(
                            jn[0:65, 0:512], gb2[:, h, :], acc[:, 512:1024], ALU.mult), (s_dv, 1))
                        spacer()
                        Eo("vector", lambda e, s=s, acc=acc, und=und, jn=jn: e.tensor_tensor(
                            und[:, SP2 * s:SP2 * s + SP2], jn[0:65, 0:512], acc[:, 0:512], ALU.add), (s_dv, 1))
                    else:
                        Eo("vector", lambda e, s=s, acc=acc, und=und: e.tensor_scalar(
                            und[:, SP2 * s:SP2 * s + SP2], acc[:, 0:512], 0.0, None, ALU.add), (s_dv, 1))
                fin_dv = s_dv.n
            Wt("scalar", s_dv)
            for h in range(2):
                und = undA if h == 0 else undB
                Eo("scalar", lambda e, und=und: e.activation(und[64:65, :], und[64:65, :], AF.Ln))
            for h in range(2):
                und = undA if h == 0 else undB
                Eo("scalar", lambda e, und=und: e.activation(und[64:65, :], und[64:65, :], AF.Exp, scale=-1.0),
                   (s_ac, 1))
            Wt("tensor", s_ac)
            for s in range(NSP2):
                for h in range(2):
                    und = undA if h == 0 else undB
                    rp = psF[0:64, 3072 + 512 * (h % 2):3072 + 512 * (h % 2) + 512]
                    Wt("tensor", s_dv, s_dv.n)
                    Eo("tensor", lambda e, s=s, und=und, o=rp: e.matmul(
                        o, onesb[64:65, 0:64], und[64:65, SP2 * s:SP2 * s + SP2], start=True, stop=True), (s_pe, 1))
                    Wt("vector", s_pe)
                    Eo("vector", lambda e, s=s, h=h, und=und, i=rp, s1b=s1b: e.scalar_tensor_tensor(
                        und[0:64, SP2 * s:SP2 * s + SP2], und[0:64, SP2 * s:SP2 * s + SP2], 1.0, i,
                        ALU.bypass, ALU.mult, accum_out=s1b[:, 8 * h + s:8 * h + s + 1]), (s_dv, 1))
            for h in range(2):
                und = undA if h == 0 else undB
                Eo("vector", lambda e, und=und, h=h, V2b=V2b, s2b=s2b: e.scalar_tensor_tensor(
                    V2b[0:64, 0:4096], und[0:64, :], 1.0, und[0:64, :], ALU.bypass, ALU.mult,
                    accum_out=s2b[:, h:h + 1]), (s_dv, 1))
            Eo("vector", lambda e, s1b=s1b, s12=s12: e.tensor_reduce(s12[:, 0:1], s1b[:, 0:8], AX.X, ALU.add))
            Eo("vector", lambda e, s1b=s1b, s12=s12: e.tensor_reduce(s12[:, 2:3], s1b[:, 8:16], AX.X, ALU.add))
            spacer()
            Eo("vector", lambda e, s2b=s2b, s12=s12: e.tensor_copy(s12[:, 1:2], s2b[:, 0:1]))
            Eo("vector", lambda e, s2b=s2b, s12=s12: e.tensor_copy(s12[:, 3:4], s2b[:, 1:2]), (s_dv, 1))
            stp = psF[:, 1024:1028]
            Wt("tensor", s_dv)
            Eo("tensor", lambda e, o=stp, s12=s12: e.matmul(o[0:1, :], onescol[0:64, :], s12,
                                                            start=True, stop=True), (s_pe, 1))
            Wt("vector", s_pe)
            Eo("vector", lambda e, p=p, i=stp: e.tensor_copy(statsrow[:, 4 * p:4 * p + 4], i[0:1, :]), (s_dv, 1))
            Wt("gpsimd", s_dv)
            Eo("gpsimd", lambda e, b=b, p=p, undA=undA: e.dma_start(attnspill[b, p, 0:64, :], undA[0:64, :]), (s_gp, 16))
            Eo("gpsimd", lambda e, b=b, p=p, undB=undB: e.dma_start(attnspill[b, p, 64:128, :], undB[0:64, :]), (s_gp, 16))
            if "att" in tapnames and b == 0 and p == 0:
                barrier(("sync",))
                Eo("sync", lambda e, gbufq=gbufq: e.dma_start(tapt["gbuf"][0:65, :], gbufq), (s_sy, 16))
                Eo("sync", lambda e, ffac=ffac: e.dma_start(tapt["ffac"][:, :], ffac), (s_sy, 16))
                Eo("sync", lambda e, biasm=biasm: e.dma_start(tapt["biasm"][:, :], biasm), (s_sy, 16))
                Eo("sync", lambda e, V2b=V2b: e.dma_start(tapt["v2"][:, :], V2b), (s_sy, 16))
                barrier(("sync",))
                Eo("sync", lambda e, undA=undA: e.dma_start(tapt["att"][0:64, :], undA[0:64, :]), (s_sy, 16))
                Eo("sync", lambda e, undB=undB: e.dma_start(tapt["att"][64:128, :], undB[0:64, :]), (s_sy, 16))
                Eo("sync", lambda e, QT=QT: e.dma_start(tapt["qkt"][0:128, 0:T], QT[:, :]), (s_sy, 16))
                Eo("sync", lambda e, KT=KT: e.dma_start(tapt["qkt"][0:128, T:2 * T], KT[:, :]), (s_sy, 16))
                Eo("sync", lambda e, Vb=Vb: e.dma_start(tapt["vb"][:, :], Vb[:, :]), (s_sy, 16))
                Eo("sync", lambda e, undA=undA: e.dma_start(tapt["nrm"][0:1, :], undA[64:65, :]), (s_sy, 16))
                Eo("sync", lambda e, crow=crow: e.dma_start(tapt["crow"][:, :], crow[:, :]), (s_sy, 16))

        def P3(b, cv):
            woTb = cv.b16(128, 8 * 1024)
            indm = cv.b16(16, 16 * 128)
            hbuf = cv.b16(128, 32 * TS)
            thb = hbuf[:, 0:T]
            attld = hbuf[:, T:2 * T]
            gld = hbuf[:, 2 * T:3 * T]
            gonP = cv.b16(128, 2 * TS)
            w1b = cv.b16(128, 2 * 8 * 128)
            w2b = cv.b16(128, 2 * 1024)
            outf = cv.f32(128, 2 * E)
            jr = cv.f32(1, 256)
            scrow = cv.f32(1, 256)
            gamr = cv.f32(1, E)
            betr = cv.f32(1, E)
            bobr = cv.f32(1, E)
            sc2 = cv.f32(128, 16)

            barrier()
            Eo("gpsimd", lambda e, woTb=woTb: e.dma_start(
                woTb.rearrange("p (c o) -> p c o", c=8),
                wT["wo"][:, :].rearrange("(c p) o -> p c o", p=128)), (s_gp, 16))
            Eo("gpsimd", lambda e, indm=indm: e.memset(indm, 1.0))
            Eo("gpsimd", lambda e, indm=indm: e.affine_select(
                indm.rearrange("p (a b) -> p a b", a=16), indm.rearrange("p (a b) -> p a b", a=16),
                pattern=[[-1, 16], [0, 128]], compare_op=ALU.is_equal, fill=0.0, base=0,
                channel_multiplier=1), (s_gc, 1))
            Eo("gpsimd", lambda e, gamr=gamr: e.dma_start(gamr, vecs["gamma"][:, :]), (s_gp, 16))
            Eo("gpsimd", lambda e, betr=betr: e.dma_start(betr, vecs["beta"][:, :]), (s_gp, 16))
            Eo("gpsimd", lambda e, bobr=bobr: e.dma_start(bobr, vecs["bo"][:, :]), (s_gp, 16))
            Wt("gpsimd", s_gp)
            Eo("gpsimd", lambda e, bobr=bobr: e.dma_start(bobr, vecs["b2"][:, :], accum_op=ALU.add), (s_gp, 16))
            NEL = float(T * HD)
            Eo("vector", lambda e, jr=jr: e.tensor_scalar(jr[:, 0:64], statsrow[:, :], float(1.0 / NEL),
                                                          None, ALU.mult), (s_dv, 1))
            mu_r = bass.AP(jr.tensor, jr.offset, [list(jr.ap[0]), [2, 16]])
            q2_r = bass.AP(jr.tensor, jr.offset + 1, [list(jr.ap[0]), [2, 16]])
            spacer()
            Eo("vector", lambda e, mu_r=mu_r, jr=jr: e.tensor_tensor(jr[:, 64:80], mu_r, mu_r, ALU.mult))
            spacer()
            Eo("vector", lambda e, q2_r=q2_r, jr=jr: e.tensor_tensor(jr[:, 80:96], q2_r, jr[:, 64:80], ALU.subtract),
               (s_dv, 1))
            Wt("scalar", s_dv)
            Eo("scalar", lambda e, jr=jr: e.activation(jr[:, 96:112], jr[:, 80:96], AF.Ln, bias=consts[0:1, 0:1]))
            Eo("scalar", lambda e, jr=jr: e.activation(jr[:, 112:128], jr[:, 96:112], AF.Exp, scale=-0.5), (s_ac, 1))
            Wt("vector", s_ac)
            Wt("vector", s_gp)
            Wt("tensor", s_gc)
            for p in range(npair):
                rsP = bass.AP(jr.tensor, jr.offset + 112 + 2 * p, [list(jr.ap[0]), [1, 2], [0, 64]])
                muP = bass.AP(jr.tensor, jr.offset + 4 * p, [list(jr.ap[0]), [2, 2], [0, 64]])
                Eo("vector", lambda e, p=p, rsP=rsP, scrow=scrow: e.tensor_tensor(
                    scrow[:, 0:128].rearrange("p (a b) -> p a b", a=2),
                    gamr[:, 128 * p:128 * p + 128].rearrange("p (a b) -> p a b", a=2), rsP, ALU.mult), (s_dv, 1))
                spacer()
                Eo("vector", lambda e, muP=muP, scrow=scrow: e.tensor_tensor(
                    scrow[:, 128:256].rearrange("p (a b) -> p a b", a=2),
                    scrow[:, 0:128].rearrange("p (a b) -> p a b", a=2), muP, ALU.mult))
                spacer()
                Eo("vector", lambda e, p=p, scrow=scrow, betr=betr: e.tensor_sub(
                    scrow[:, 128:256], betr[:, 128 * p:128 * p + 128], scrow[:, 128:256]), (s_dv, 1))
                scp = psF[:, 3072 + 512 * (p % 2):3072 + 512 * (p % 2) + 2]
                Wt("tensor", s_dv)
                Eo("tensor", lambda e, o=scp, scrow=scrow: e.matmul(
                    o[:, 0:1], scrow[0:1, 0:128], onesf[0:1, 0:1], start=True, stop=True))
                Eo("tensor", lambda e, o=scp, scrow=scrow: e.matmul(
                    o[:, 1:2], scrow[0:1, 128:256], onesf[0:1, 0:1], start=True, stop=True), (s_pe, 1))
                Wt("vector", s_pe)
                Eo("vector", lambda e, p=p, i=scp, sc2=sc2: e.tensor_copy(sc2[:, 2 * p:2 * p + 2], i), (s_dv, 1))
            if "sc2" in tapnames and b == 0:
                barrier(("sync",))
                Eo("sync", lambda e, sc2=sc2: e.dma_start(tapt["sc2"][:, :], sc2), (s_sy, 16))
                Eo("sync", lambda e, jr=jr: e.dma_start(tapt["jr"][:, :], jr), (s_sy, 16))
                Eo("sync", lambda e: e.dma_start(tapt["strow"][:, :], statsrow[:, :]), (s_sy, 16))
            for p in range(npair):
                Wt("sync", s_gp, s_gp.n)
                Wt("sync", s_dv, s_dv.n)
                Eo("sync", lambda e, b=b, p=p, attld=attld: e.dma_start(attld, attnspill[b, p, :, :]), (s_sy, 16))
                Eo("sync", lambda e, b=b, p=p, gld=gld: e.dma_start(gld, gatespill[b, p, :, :]), (s_sy, 16))
                Wt("scalar", s_sy)
                Eo("scalar", lambda e, gld=gld, thb=thb: e.activation(thb, gld, AF.Tanh, scale=0.5), (s_ac, 1))
                Wt("vector", s_ac)
                Eo("vector", lambda e, p=p, attld=attld, sc2=sc2: e.tensor_scalar(
                    attld, attld, sc2[:, 2 * p:2 * p + 1], sc2[:, 2 * p + 1:2 * p + 2], ALU.mult, ALU.add))
                Eo("vector", lambda e, thb=thb: e.tensor_scalar(thb, thb, 1.0, 0.5, ALU.add, ALU.mult))
                spacer()
                Eo("vector", lambda e, thb=thb, gld=gld: e.tensor_mul(thb, thb, gld))
                spacer()
                Eo("vector", lambda e, thb=thb, attld=attld, gld=gld: e.tensor_mul(gld, thb, attld), (s_dv, 1))
                Wt("gpsimd", s_dv)
                Eo("gpsimd", lambda e, b=b, p=p, gld=gld: e.dma_start(gonspill[b, p, :, :], gld), (s_gp, 16))
            if "gon0" in tapnames and b == 0:
                barrier(("sync",))
                Eo("sync", lambda e, b=b: e.dma_start(tapt["gon0"][:, :], gonspill[b, 0, :, :]), (s_sy, 16))
                Eo("sync", lambda e, b=b: e.dma_start(tapt["asp0"][:, :], attnspill[b, 0, :, :]), (s_sy, 16))
                Eo("sync", lambda e, b=b: e.dma_start(tapt["gsp0"][:, :], gatespill[b, 0, :, :]), (s_sy, 16))
            w1free = {0: 0, 1: 0}
            w2free = {0: 0, 1: 0}
            gfree = {0: 0, 1: 0}
            ofree = {0: 0, 1: 0}
            xfree = {0: 0, 1: 0}
            hpfree = {}
            mbfree = {}
            for ts_ in range(NTS):
                for och in range(32):
                    wsl = och % 2
                    Wt("gpsimd", s_pe, w1free[wsl])
                    Eo("gpsimd", lambda e, och=och, wsl=wsl, w1b=w1b: e.dma_start(
                        w1b.rearrange("p (s c o) -> p s c o", s=2, c=8)[:, wsl, :, :],
                        w1T[:, 128 * och:128 * och + 128].rearrange("(c q) o -> q c o", q=128)), (s_gp, 16))
                    need_gp = s_gp.n
                    if och % 2 == 0:
                        blk = och // 2
                        mbp = psF[:, 1024 + 512 * (blk % 2):1024 + 512 * (blk % 2) + 512]
                        Wt("tensor", s_dv, mbfree.get(blk % 2, 0))
                        Eo("tensor", lambda e, blk=blk, ts_=ts_, o=mbp, indm=indm: e.matmul(
                            o, indm.rearrange("p (a b) -> p a b", a=16)[:, blk, :],
                            maskT[:, TS * ts_:TS * ts_ + TS], start=True, stop=True), (s_pe, 1))
                    hp = psF[:, 512 * (och % 2):512 * (och % 2) + 512]
                    Wt("tensor", s_gp, need_gp)
                    Wt("tensor", s_dv, hpfree.get(och % 2, 0))
                    for c in range(8):
                        Eo("tensor", lambda e, c=c, och=och, ts_=ts_, wsl=wsl, o=hp, w1b=w1b: e.matmul(
                            o, w1b.rearrange("p (s c o) -> p s c o", s=2, c=8)[:, wsl, c, :],
                            xnT[:, c, TS * ts_:TS * ts_ + TS], start=(c == 0), stop=(c == 7)),
                           *([(s_pe, 1)] if c == 7 else []))
                    w1free[wsl] = s_pe.n
                    Wt("vector", s_pe)
                    Eo("vector", lambda e, och=och, i=hp, hbuf=hbuf: e.tensor_scalar(
                        hbuf.rearrange("p (o n) -> p o n", o=32)[:, och, :], i, b1cols[:, och:och + 1], 0.0,
                        ALU.add, ALU.max), (s_dv, 1))
                    hpfree[och % 2] = s_dv.n
                    mbp2 = psF[:, 1024 + 512 * ((och // 2) % 2):1024 + 512 * ((och // 2) % 2) + 512]
                    spacer()
                    Eo("vector", lambda e, och=och, i=mbp2, hbuf=hbuf: e.tensor_tensor(
                        hbuf.rearrange("p (o n) -> p o n", o=32)[:, och, :],
                        hbuf.rearrange("p (o n) -> p o n", o=32)[:, och, :], i, ALU.mult), (s_dv, 1))
                    if och % 2 == 1:
                        mbfree[(och // 2) % 2] = s_dv.n
                hdone = s_dv.n
                Wt("tensor", s_dv, hdone)
                for p in range(npair):
                    gsl = p % 2
                    Wt("gpsimd", s_pe, gfree[gsl])
                    Eo("gpsimd", lambda e, b=b, p=p, ts_=ts_, gsl=gsl, gonP=gonP: e.dma_start(
                        gonP.rearrange("p (s n) -> p s n", s=2)[:, gsl, :],
                        gonspill[b, p, :, TS * ts_:TS * ts_ + TS]), (s_gp, 16))
                    Wt("tensor", s_gp)
                    for tt_ in range(4):
                        for dsp in range(2):
                            slot = psF[:, 512 * (2 * tt_ + dsp):512 * (2 * tt_ + dsp) + 512]
                            Eo("tensor", lambda e, p=p, tt_=tt_, dsp=dsp, o=slot, gsl=gsl, gonP=gonP, woTb=woTb: e.matmul(
                                o, gonP.rearrange("p (s n) -> p s n", s=2)[:, gsl, 128 * tt_:128 * tt_ + 128],
                                woTb.rearrange("p (c o) -> p c o", c=8)[:, p, 512 * dsp:512 * dsp + 512],
                                start=(p == 0), stop=False),
                               *([(s_pe, 1)] if (tt_ == 3 and dsp == 1) else []))
                    gfree[gsl] = s_pe.n
                for och in range(32):
                    wsl = och % 2
                    Wt("gpsimd", s_pe, w2free[wsl])
                    Eo("gpsimd", lambda e, och=och, wsl=wsl, w2b=w2b: e.dma_start(
                        w2b.rearrange("p (s n) -> p s n", s=2)[:, wsl, :],
                        w2T[128 * och:128 * och + 128, :]), (s_gp, 16))
                    Wt("tensor", s_gp)
                    for tt_ in range(4):
                        for dsp in range(2):
                            slot = psF[:, 512 * (2 * tt_ + dsp):512 * (2 * tt_ + dsp) + 512]
                            Eo("tensor", lambda e, och=och, tt_=tt_, dsp=dsp, o=slot, wsl=wsl, hbuf=hbuf, w2b=w2b: e.matmul(
                                o, hbuf.rearrange("p (o n) -> p o n", o=32)[:, och, 128 * tt_:128 * tt_ + 128],
                                w2b.rearrange("p (s n) -> p s n", s=2)[:, wsl, 512 * dsp:512 * dsp + 512],
                                start=False, stop=False),
                               *([(s_pe, 1)] if (tt_ == 3 and dsp == 1) else []))
                    w2free[wsl] = s_pe.n
                for tt_ in range(4):
                    for dsp in range(2):
                        slot = psF[:, 512 * (2 * tt_ + dsp):512 * (2 * tt_ + dsp) + 512]
                        Eo("tensor", lambda e, tt_=tt_, dsp=dsp, o=slot, bobr=bobr: e.matmul(
                            o, onesf[:, :], bobr[:, 512 * dsp:512 * dsp + 512],
                            start=False, stop=True),
                           *([(s_pe, 1)] if (tt_ == 3 and dsp == 1) else []))
                ydone = s_pe.n
                for tt_ in range(4):
                    tg = 4 * ts_ + tt_
                    xsl = tt_ % 2
                    Wt("sync", s_dv, xfree[xsl])
                    Eo("sync", lambda e, b=b, tg=tg, xsl=xsl: e.dma_start(
                        xtile[:, xsl, :], xin[b, 128 * tg:128 * tg + 128, :]), (s_sy, 16))
                    xload = s_sy.n
                    Wt("vector", s_sy, xload)
                    Wt("vector", s_pe, ydone)
                    Wt("vector", s_sy, ofree[tt_ % 2])
                    for dsp in range(2):
                        slot = psF[:, 512 * (2 * tt_ + dsp):512 * (2 * tt_ + dsp) + 512]
                        Eo("vector", lambda e, tt_=tt_, dsp=dsp, xsl=xsl, i=slot, outf=outf: e.tensor_tensor(
                            outf.rearrange("p (s n) -> p s n", s=2)[:, tt_ % 2, 512 * dsp:512 * dsp + 512],
                            xtile[:, xsl, 512 * dsp:512 * dsp + 512], i, ALU.add), (s_dv, 1))
                    xfree[xsl] = s_dv.n
                    Wt("sync", s_dv)
                    Eo("sync", lambda e, b=b, tg=tg, tt_=tt_, outf=outf: e.dma_start(
                        out[b, 128 * tg:128 * tg + 128, :],
                        outf.rearrange("p (s n) -> p s n", s=2)[:, tt_ % 2, :]), (s_sy, 16))
                    ofree[tt_ % 2] = s_sy.n
                hpfree = {0: s_dv.n, 1: s_dv.n}
                mbfree = {0: s_dv.n, 1: s_dv.n}

        P0()
        for b in range(nb):
            cv = Carver(AR)
            P1(b, cv)
            cv = Carver(AR)
            costab = cv.f16(128, NTT * 64)
            sinsig = cv.f16(128, NTT * 64)
            tsc = cv.f32(128, 512)
            cv2_base = cv.off
            barrier()
            trig(costab.rearrange("p (a b) -> p a b", a=NTT),
                 sinsig.rearrange("p (a b) -> p a b", a=NTT), tsc)
            if "cost" in tapnames and b == 0:
                barrier(("sync",))
                Eo("sync", lambda e, costab=costab: e.dma_start(tapt["cost"][:, :], costab), (s_sy, 16))
                Eo("sync", lambda e, sinsig=sinsig: e.dma_start(tapt["sins"][:, :], sinsig), (s_sy, 16))
            for p in range(npair):
                cv.off = cv2_base
                P2(b, p, cv, costab.rearrange("p (a b) -> p a b", a=NTT),
                   sinsig.rearrange("p (a b) -> p a b", a=NTT))
            if stage >= 3:
                cv = Carver(AR)
                P3(b, cv)
        barrier(("sync", "gpsimd"))

        block = st.enter_context(nc.Block())

        def mk(engname):
            def run(eng):
                for fn, incs in prog[engname]:
                    ins = fn(eng)
                    for hh, k in incs:
                        ins = ins.then_inc(hh, k)
            return run

        block.tensor(mk("tensor"))
        block.vector(mk("vector"))
        block.scalar(mk("scalar"))
        block.gpsimd(mk("gpsimd"))
        block.sync(mk("sync"))

    return nc


def _stage_inputs(inputs):
    ins = {}
    ins["x"] = np.ascontiguousarray(inputs["L0_attn_x"])
    for nm in ("wq", "wk", "wv", "wg", "wo"):
        ins[nm + "T"] = np.ascontiguousarray(inputs["L0_attn_" + nm].T)
    ins["w1T"] = np.ascontiguousarray(inputs["L0_ffn_w1"].T)
    ins["w2T"] = np.ascontiguousarray(inputs["L0_ffn_w2"].T)
    ins["wdT"] = np.ascontiguousarray(inputs["L0_attn_wd"].T)
    ins["rwT"] = np.ascontiguousarray(inputs["L0_ffn_router_w"].T)
    for src, dst in (("L0_attn_bq", "bq"), ("L0_attn_bk", "bk"), ("L0_attn_bv", "bv"),
                     ("L0_attn_bg", "bg"), ("L0_attn_bo", "bo"), ("L0_attn_bd", "bd"),
                     ("L0_ffn_b1", "b1"), ("L0_ffn_b2", "b2"), ("L0_ffn_router_b", "rb"),
                     ("L0_attn_gamma", "gamma"), ("L0_attn_beta", "beta")):
        ins[dst] = np.ascontiguousarray(inputs[src]).reshape(1, -1)
    return ins


def kernel(**inputs):
    ins = _stage_inputs(inputs)
    x = ins.pop("x")
    nc = build()
    in_maps = []
    for i in range(8):
        m = dict(ins)
        m["x"] = np.ascontiguousarray(x[2 * i:2 * i + 2])
        in_maps.append(m)
    res = run_bass_kernel_spmd(nc, in_maps, core_ids=list(range(8)))
    return np.concatenate([r["out"] for r in res.results], axis=0)


# version 13
# speedup vs baseline: 1.4980x; 1.4980x over previous
import math
import numpy as np
import ml_dtypes
from contextlib import ExitStack

import concourse.bass as bass
import concourse.mybir as mybir
from concourse.bass_utils import run_bass_kernel_spmd

F32 = mybir.dt.float32
BF16 = mybir.dt.bfloat16
F16 = mybir.dt.float16
I32 = mybir.dt.int32
AF = mybir.ActivationFunctionType
ALU = mybir.AluOpType
AX = mybir.AxisListType

B, T, E, H, HD = 2, 4096, 1024, 16, 64
NPAIR = 8
SP, NSP = 256, 16
SP2, NSP2 = 512, 8
NG, CPG = 8, 4
NTT = 32
TS, NTS = 512, 8
EPS = 1e-5

def _trunc12(v):
    i = np.array(np.float32(v), np.float32).view(np.uint32)
    i = np.uint32(int(i) & ~0xFFF)
    return float(np.array(i, np.uint32).view(np.float32))
TWO_PI = 2.0 * math.pi
CW1 = _trunc12(TWO_PI)
CW2 = _trunc12(TWO_PI - CW1)
CW3 = float(np.float32(TWO_PI - CW1 - CW2))
INV2PI = float(np.float32(1.0 / TWO_PI))

ENGS = ("tensor", "vector", "scalar", "gpsimd", "sync")
ARENA_N = 25000


def rep(ap, n, pos=1):
    a = [list(d) for d in ap.ap]
    a.insert(pos, [0, n])
    return bass.AP(ap.tensor, ap.offset, a)


class Carver:
    def __init__(self, ar, off=0):
        self.ar = ar
        self.off = off

    def f32(self, parts, n):
        ap = self.ar[0:parts, self.off:self.off + n]
        self.off += n
        assert self.off <= ARENA_N, f"arena overflow {self.off}"
        return ap

    def b16(self, parts, n):
        assert n % 2 == 0
        ap = self.ar[0:parts, self.off:self.off + n // 2].bitcast(BF16)
        self.off += n // 2
        assert self.off <= ARENA_N, f"arena overflow {self.off}"
        return ap

    def f16(self, parts, n):
        assert n % 2 == 0
        ap = self.ar[0:parts, self.off:self.off + n // 2].bitcast(F16)
        self.off += n // 2
        return ap


class Sem:
    def __init__(self, h):
        self.h = h
        self.n = 0


def build(nb=B, npair=NPAIR, taps=(), stage=3):
    nc = bass.Bass("TRN2", target_bir_lowering=False)

    xin = nc.dram_tensor("x", [nb, T, E], F32, kind="ExternalInput")
    wT = {nm: nc.dram_tensor(nm + "T", [E, E], F32, kind="ExternalInput")
          for nm in ("wq", "wk", "wv", "wg", "wo")}
    w1T = nc.dram_tensor("w1T", [E, 4096], F32, kind="ExternalInput")
    w2T = nc.dram_tensor("w2T", [4096, E], F32, kind="ExternalInput")
    wdT = nc.dram_tensor("wdT", [E, 16], F32, kind="ExternalInput")
    rwT = nc.dram_tensor("rwT", [E, 16], F32, kind="ExternalInput")
    vecs = {}
    for nm, n in (("bq", E), ("bk", E), ("bv", E), ("bg", E), ("bo", E), ("bd", 16),
                  ("b1", 4096), ("b2", E), ("rb", 16), ("gamma", E), ("beta", E)):
        vecs[nm] = nc.dram_tensor(nm, [1, n], F32, kind="ExternalInput")
    out = nc.dram_tensor("out", [nb, T, E], F32, kind="ExternalOutput")
    attnspill = nc.dram_tensor("attnspill", [nb, NPAIR, 128, T], BF16)
    gatespill = nc.dram_tensor("gatespill", [nb, NPAIR, 128, T], BF16)
    gonspill = nc.dram_tensor("gonspill", [nb, NPAIR, 128, T], BF16)
    tapt = {}
    for tname, shape, dt in taps:
        tapt[tname] = nc.dram_tensor("tap_" + tname, shape, dt, kind="ExternalOutput")
    tapnames = set(t[0] for t in taps)

    prog = {e: [] for e in ENGS}

    def Eo(eng, fn, *incs):
        for s, k in incs:
            s.n += k
        prog[eng].append((fn, tuple((s.h, k) for s, k in incs)))

    def Wt(eng, sem, val=None):
        v = sem.n if val is None else val
        if v > 0:
            prog[eng].append((lambda e, _h=sem.h, _v=v: e.wait_ge(_h, _v), ()))

    with ExitStack() as st:
        def sbt(name, shape, dt):
            return st.enter_context(nc.sbuf_tensor(name, shape, dt))

        ident = sbt("ident", [128, 128], BF16)
        identf = sbt("identf", [128, 128], F32)
        onesf = sbt("onesf", [1, 128], F32)
        onesb = sbt("onesb", [65, 128], BF16)
        onescol = sbt("onescol", [128, 1], F32)
        consts = sbt("consts", [128, 8], F32)
        rwTf = sbt("rwTf", [128, 8, 16], F32)
        wdTf = sbt("wdTf", [128, 8, 16], F32)
        b1cols = sbt("b1cols", [128, 32], F32)
        bgcols = sbt("bgcols", [128, 8], F32)
        bdrow = sbt("bdrow", [1, 16], F32)
        rbrow = sbt("rbrow", [1, 16], F32)
        kmod4 = sbt("kmod4", [128, 2], F32)
        giota = sbt("giota", [128, 32], F32)
        siota = sbt("siota", [128, 8], F32)
        fiota = sbt("fiota", [1, 512], F32)
        trowb = sbt("trowb", [1, 128], F32)
        invfemb = sbt("invfemb", [1, 64], F32)
        DpatT = sbt("DpatT", [128, 4, 512], F16)
        brows = {nm: sbt(nm + "r", [1, E], BF16) for nm in ("bq", "bk", "bv")}
        xnT = sbt("xnT", [128, 8, T], BF16)
        maskT = sbt("maskT", [16, T], BF16)
        stats = sbt("stats", [128, 4, NTT], F32)
        pdacc = sbt("pdacc", [128, 8, NTT], F32)
        pdmean = sbt("pdmean", [128, 8], F32)
        ldb = sbt("ldb", [128, 16], F32)
        ldrow = sbt("ldrow", [1, 16], F32)
        negld = sbt("negld", [1, 16], F32)
        pdrow = sbt("pdrow", [1, 16], F32)
        rowtmp = sbt("rowtmp", [1, 16], F32)
        statsrow = sbt("statsrow", [1, 64], F32)
        lgbuf = sbt("lgbuf", [128, 3, 16], F32)
        m4col = sbt("m4col", [128, 2], F32)
        maskb = sbt("maskb", [128, 16], BF16)
        xtile = sbt("xtile", [128, 4, E], F32)
        spc = sbt("spc", [128, 8], F32)
        dbglg = sbt("dbglg", [128, 512], F32)
        AR = sbt("AR", [128, ARENA_N], F32)

        psF = st.enter_context(nc.psum_tensor("psbig", [128, 4096], F32))
        psH = psF[:, :].bitcast(BF16)

        s_gp = Sem(st.enter_context(nc.semaphore("s_gp")))
        s_sy = Sem(st.enter_context(nc.semaphore("s_sy")))
        s_pe = Sem(st.enter_context(nc.semaphore("s_pe")))
        s_dv = Sem(st.enter_context(nc.semaphore("s_dv")))
        s_ac = Sem(st.enter_context(nc.semaphore("s_ac")))
        s_gc = Sem(st.enter_context(nc.semaphore("s_gc")))
        ALLS = (s_gp, s_sy, s_pe, s_dv, s_ac, s_gc)

        def barrier(engs=ENGS):
            for e in engs:
                for s in ALLS:
                    Wt(e, s)

        def spacer():
            Eo("vector", lambda e: e.tensor_copy(spc[:, 0:4], consts[:, 0:4]))

        def P0():
            g = "gpsimd"
            Eo(g, lambda e: e.memset(ident[:, :], 1.0))
            Eo(g, lambda e: e.affine_select(ident[:, :], ident[:, :], pattern=[[-1, 128]],
                                            compare_op=ALU.is_equal, fill=0.0, base=0, channel_multiplier=1))
            Eo(g, lambda e: e.memset(identf[:, :], 1.0))
            Eo(g, lambda e: e.affine_select(identf[:, :], identf[:, :], pattern=[[-1, 128]],
                                            compare_op=ALU.is_equal, fill=0.0, base=0, channel_multiplier=1))
            Eo(g, lambda e: e.memset(onesf[:, :], 1.0))
            Eo(g, lambda e: e.memset(onesb[:, :], 1.0))
            Eo(g, lambda e: e.memset(onescol[:, :], 1.0))
            Eo(g, lambda e: e.memset(consts[:, :], 0.0))
            Eo(g, lambda e: e.memset(consts[:, 0:1], float(EPS)))
            Eo(g, lambda e: e.memset(consts[:, 1:2], float(-math.log(8.0) + math.log(1.02))))
            Eo(g, lambda e: e.memset(consts[:, 2:3], float(-5.0 * math.log(2.0))))
            Eo(g, lambda e: e.iota(kmod4[:, :], pattern=[[128, 2]], base=0,
                                   channel_multiplier=1, allow_small_or_imprecise_dtypes=True))
            Eo(g, lambda e: e.iota(giota[:, :], pattern=[[128, 32]], base=0,
                                   channel_multiplier=0, allow_small_or_imprecise_dtypes=True))
            Eo(g, lambda e: e.iota(siota[:, :], pattern=[[512, 8]], base=0,
                                   channel_multiplier=0, allow_small_or_imprecise_dtypes=True))
            Eo(g, lambda e: e.iota(fiota[:, :], pattern=[[1, 512]], base=0,
                                   channel_multiplier=0, allow_small_or_imprecise_dtypes=True))
            Eo(g, lambda e: e.iota(
                DpatT[:, :, :],
                pattern=[[128, 4], [-1, 512]], base=0,
                channel_multiplier=1, allow_small_or_imprecise_dtypes=True))
            Eo(g, lambda e: e.iota(invfemb[:, :].rearrange("p (a b) -> p a b", a=2),
                                   pattern=[[0, 2], [1, 32]], base=0,
                                   channel_multiplier=0, allow_small_or_imprecise_dtypes=True), (s_gc, 1))
            Eo(g, lambda e: e.dma_start(rwTf[:, :, :], rwT[:, :].rearrange("(c p) o -> p c o", p=128)), (s_gp, 16))
            Eo(g, lambda e: e.dma_start(wdTf[:, :, :], wdT[:, :].rearrange("(c p) o -> p c o", p=128)), (s_gp, 16))
            for c in range(32):
                Eo(g, lambda e, c=c: e.dma_start(b1cols[:, c:c + 1],
                    vecs["b1"][0:1, 128 * c:128 * c + 128].rearrange("q (p a) -> (q p) a", a=1)), (s_gp, 16))
            for c in range(8):
                Eo(g, lambda e, c=c: e.dma_start(bgcols[:, c:c + 1],
                    vecs["bg"][0:1, 128 * c:128 * c + 128].rearrange("q (p a) -> (q p) a", a=1)), (s_gp, 16))
            Eo(g, lambda e: e.dma_start(bdrow[:, :], vecs["bd"][:, :]), (s_gp, 16))
            Eo(g, lambda e: e.dma_start(rbrow[:, :], vecs["rb"][:, :]), (s_gp, 16))
            for nm in ("bq", "bk", "bv"):
                Eo(g, lambda e, nm=nm: e.dma_start(brows[nm][:, :], vecs[nm][:, :]), (s_gp, 16))
            Wt("vector", s_gc)
            Eo("vector", lambda e: e.tensor_scalar(
                DpatT[:, :, :], DpatT[:, :, :], 0.0, None, ALU.max), (s_dv, 1))
            Wt("scalar", s_gc)
            Eo("scalar", lambda e: e.activation(invfemb[:, :], invfemb[:, :], AF.Exp,
                                                scale=float(-math.log(10000.0) / 32.0)), (s_ac, 1))

        def trig(costab, sinsig, tsc):
            for tt in range(NTT):
                Wt("gpsimd", s_pe, s_pe.n)
                Eo("gpsimd", lambda e, tt=tt: e.iota(trowb[:, :], pattern=[[1, 128]], base=128 * tt,
                                                     channel_multiplier=0, allow_small_or_imprecise_dtypes=True),
                   (s_gc, 1))
                slot = tt % 2
                ps_th = psF[:, 3072 + 512 * slot:3072 + 512 * slot + 64]
                Wt("tensor", s_gc)
                Wt("tensor", s_ac)
                Eo("tensor", lambda e, o=ps_th: e.matmul(o, trowb[:, :], invfemb[:, :],
                                                         start=True, stop=True), (s_pe, 1))
                Wt("vector", s_pe)
                for wh in range(2):
                    off = 0.0 if wh == 0 else math.pi / 2.0
                    t0 = tsc[:, 256 * wh:256 * wh + 64]
                    t1 = tsc[:, 256 * wh + 64:256 * wh + 128]
                    t2 = tsc[:, 256 * wh + 128:256 * wh + 192]
                    t3 = tsc[:, 256 * wh + 192:256 * wh + 256]
                    Eo("vector", lambda e, o=ps_th, off=off, t0=t0: e.tensor_scalar(
                        t0, o, float(INV2PI), float(off * INV2PI), ALU.mult, ALU.add))
                    spacer()
                    Eo("vector", lambda e, t0=t0, t1=t1: e.tensor_copy(t1.bitcast(I32)[:, 0:64], t0))
                    spacer()
                    Eo("vector", lambda e, t1=t1, t2=t2: e.tensor_copy(t2, t1.bitcast(I32)[:, 0:64]))
                    spacer()
                    Eo("vector", lambda e, o=ps_th, t2=t2, t0=t0: e.scalar_tensor_tensor(
                        t0, t2, float(-CW1), o, ALU.mult, ALU.add))
                    spacer()
                    if off:
                        Eo("vector", lambda e, off=off, t0=t0: e.tensor_scalar(
                            t0, t0, float(off), None, ALU.add))
                        spacer()
                    Eo("vector", lambda e, t2=t2, t0=t0, t3=t3: e.scalar_tensor_tensor(
                        t3, t2, float(-CW2), t0, ALU.mult, ALU.add))
                    spacer()
                    Eo("vector", lambda e, t2=t2, t3=t3, t0=t0: e.scalar_tensor_tensor(
                        t0, t2, float(-CW3), t3, ALU.mult, ALU.add), (s_dv, 1))
                    Wt("scalar", s_dv)
                    dst = sinsig if wh == 0 else costab
                    Eo("scalar", lambda e, tt=tt, dst=dst, t0=t0: e.activation(
                        dst[:, tt, :], t0, AF.Sin), (s_ac, 1))
                Wt("vector", s_ac)
                ev = sinsig[:, tt, :].rearrange("p (a b) -> p a b", b=2)[:, :, 0:1]
                Eo("vector", lambda e, ev=ev: e.tensor_scalar(ev, ev, -1.0, None, ALU.mult), (s_dv, 1))

        def P1(b, cv):
            xnf = cv.f32(128, 1024)
            xnb = cv.b16(128, 1024)
            junk = cv.f32(128, 1024)
            xfT = cv.f32(128, 256)
            j2 = cv.f32(128, 128)
            xfree = {}
            xneed = {}

            def issue_x(tt):
                slot = tt % 4
                Wt("sync", s_dv, xfree.get(slot, 0))
                Eo("sync", lambda e, b=b, tt=tt, slot=slot: e.dma_start(
                    xtile[:, slot, :], xin[b, 128 * tt:128 * tt + 128, :]), (s_sy, 16))
                xneed[tt] = s_sy.n
            for tt in range(4):
                issue_x(tt)
            for tt in range(NTT):
                slot = tt % 4
                Wt("vector", s_sy, xneed[tt])
                Eo("vector", lambda e, tt=tt, slot=slot, junk=junk: e.tensor_scalar(
                    junk, xtile[:, slot, :], 1.0, 0.0, ALU.mult, ALU.add, accum_out=stats[:, 0, tt:tt + 1]))
                Eo("vector", lambda e, tt=tt, slot=slot, junk=junk: e.scalar_tensor_tensor(
                    junk, xtile[:, slot, :], 1.0, xtile[:, slot, :], ALU.bypass, ALU.mult,
                    accum_out=stats[:, 1, tt:tt + 1]), (s_dv, 1))
                xfree[slot] = s_dv.n
                if tt + 4 < NTT:
                    issue_x(tt + 4)
            Eo("vector", lambda e: e.tensor_scalar(stats[:, 2, :], stats[:, 0, :], float(1.0 / E), None, ALU.mult))
            Eo("vector", lambda e, junk=junk: e.tensor_scalar(junk[:, 0:NTT], stats[:, 1, :], float(1.0 / E), None, ALU.mult))
            Eo("vector", lambda e, junk=junk: e.tensor_mul(junk[:, NTT:2 * NTT], stats[:, 2, :], stats[:, 2, :]))
            spacer()
            Eo("vector", lambda e, junk=junk: e.tensor_sub(stats[:, 3, :], junk[:, 0:NTT], junk[:, NTT:2 * NTT]), (s_dv, 1))
            Wt("scalar", s_dv)
            Eo("scalar", lambda e, junk=junk: e.activation(junk[:, 0:NTT], stats[:, 3, :], AF.Ln, bias=consts[:, 0:1]))
            Eo("scalar", lambda e, junk=junk: e.activation(stats[:, 3, :], junk[:, 0:NTT], AF.Exp, scale=-0.5), (s_ac, 1))
            lgT = psF[:, 1024:1152]
            lgtm = psF[:, 1536:1552]
            mtp = psH[:, 4096:4224]
            for tt in range(4):
                issue_x(tt)
            for tt in range(NTT):
                slot = tt % 4
                Wt("vector", s_sy, xneed[tt])
                Wt("vector", s_ac)
                Eo("vector", lambda e, tt=tt, slot=slot, xnf=xnf: e.tensor_scalar(
                    xnf, xtile[:, slot, :], stats[:, 2, tt:tt + 1], stats[:, 3, tt:tt + 1],
                    ALU.subtract, ALU.mult), (s_dv, 1))
                xfree[slot] = s_dv.n
                if tt + 4 < NTT:
                    issue_x(tt + 4)
                spacer()
                Eo("vector", lambda e, xnf=xnf, xnb=xnb: e.tensor_copy(xnb, xnf), (s_dv, 1))
                for c in range(8):
                    pstr = psH[:, 1024 * (c % 2):1024 * (c % 2) + 128]
                    psxf = psF[:, 3072 + 512 * (c % 2):3072 + 512 * (c % 2) + 128]
                    Wt("tensor", s_dv)
                    Eo("tensor", lambda e, c=c, o=pstr, xnb=xnb: e.transpose(
                        o, xnb[:, 128 * c:128 * c + 128], ident[:, :]))
                    Eo("tensor", lambda e, c=c, o=psxf, xnf=xnf: e.transpose(
                        o, xnf[:, 128 * c:128 * c + 128], identf[:, :]), (s_pe, 1))
                    Wt("vector", s_pe)
                    Eo("vector", lambda e, c=c, tt=tt, i=pstr: e.tensor_scalar(
                        xnT[:, c, 128 * tt:128 * tt + 128], i, 1.0, 0.0, ALU.mult, ALU.add,
                        accum_out=pdacc[:, c, tt:tt + 1]))
                    Eo("vector", lambda e, c=c, i=psxf, xfT=xfT: e.tensor_copy(
                        xfT[:, 128 * (c % 2):128 * (c % 2) + 128], i), (s_dv, 1))
                    Wt("tensor", s_dv)
                    Eo("tensor", lambda e, c=c, o=lgT, xfT=xfT: e.matmul(
                        o[0:16, :], rwTf[:, c, :], xfT[:, 128 * (c % 2):128 * (c % 2) + 128],
                        start=(c == 0), stop=False))
                Eo("tensor", lambda e, o=lgT: e.matmul(o[0:16, :], rbrow[:, :], onesf[:, :],
                                                       start=False, stop=True), (s_pe, 1))
                Wt("vector", s_pe)
                Eo("vector", lambda e, o=lgT, j2=j2: e.tensor_copy(j2[0:16, 0:128], o[0:16, :]), (s_dv, 1))
                Wt("tensor", s_dv)
                Eo("tensor", lambda e, o=lgtm, j2=j2: e.transpose(o, j2[0:16, 0:128], identf[0:16, 0:16]), (s_pe, 1))
                Wt("vector", s_pe)
                Eo("vector", lambda e, i=lgtm: e.tensor_copy(lgbuf[:, 0, :], i))
                Eo("vector", lambda e, i=lgtm: e.tensor_copy(lgbuf[:, 1, :], i), (s_dv, 1))
                Eo("vector", lambda e, tt=tt, i=lgtm: e.tensor_copy(dbglg[:, 16 * tt:16 * tt + 16], i))
                for it in range(3):
                    spacer()
                    Eo("vector", lambda e: e.reduce_max(m4col[:, 0:1], lgbuf[:, 1, :], AX.X))
                    spacer()
                    Eo("vector", lambda e: e.tensor_single_scalar(lgbuf[:, 2, :], lgbuf[:, 1, :], m4col[:, 0:1], ALU.is_ge))
                    spacer()
                    Eo("vector", lambda e: e.scalar_tensor_tensor(
                        lgbuf[:, 1, :], lgbuf[:, 2, :], -1e30, lgbuf[:, 1, :], ALU.mult, ALU.add))
                spacer()
                Eo("vector", lambda e: e.reduce_max(m4col[:, 1:2], lgbuf[:, 1, :], AX.X))
                spacer()
                Eo("vector", lambda e: e.tensor_single_scalar(maskb[:, :], lgbuf[:, 0, :], m4col[:, 1:2], ALU.is_ge),
                   (s_dv, 1))
                Wt("tensor", s_dv)
                Eo("tensor", lambda e, o=mtp: e.transpose(o[0:16, :], maskb[:, :], ident[:, :]), (s_pe, 1))
                Wt("vector", s_pe)
                Eo("vector", lambda e, tt=tt, i=mtp: e.tensor_copy(
                    maskT[:, 128 * tt:128 * tt + 128], i[0:16, :]), (s_dv, 1))
            Eo("vector", lambda e: e.tensor_reduce(pdmean[:, :], pdacc[:, :, :], AX.X, ALU.add))
            spacer()
            Eo("vector", lambda e: e.tensor_scalar(pdmean[:, :], pdmean[:, :], float(1.0 / T), None, ALU.mult), (s_dv, 1))
            pdp = psF[:, 1664:1680]
            Wt("tensor", s_dv)
            for c in range(8):
                Eo("tensor", lambda e, c=c, o=pdp: e.matmul(o[0:1, :], pdmean[:, c:c + 1], wdTf[:, c, :],
                                                            start=(c == 0), stop=False))
            Eo("tensor", lambda e, o=pdp: e.matmul(o[0:1, :], onesf[:, 0:1], bdrow[:, :],
                                                   start=False, stop=True), (s_pe, 1))
            Wt("scalar", s_pe)
            Eo("scalar", lambda e, i=pdp: e.activation(rowtmp[:, :], i[0:1, :], AF.Tanh, scale=0.5), (s_ac, 1))
            Wt("vector", s_ac)
            Eo("vector", lambda e: e.tensor_scalar(pdrow[:, :], rowtmp[:, :], 0.5, 0.5, ALU.mult, ALU.add), (s_dv, 1))
            Wt("scalar", s_dv)
            Eo("scalar", lambda e: e.activation(rowtmp[:, :], pdrow[:, :], AF.Exp,
                                                scale=float(-8.0 * math.log(2.0)), bias=consts[0:1, 2:3]), (s_ac, 1))
            Wt("vector", s_ac)
            Eo("vector", lambda e: e.tensor_scalar(ldrow[:, :], rowtmp[:, :], -1.0, 1.0, ALU.mult, ALU.add), (s_dv, 1))
            Wt("scalar", s_dv)
            Eo("scalar", lambda e: e.activation(ldrow[:, :], ldrow[:, :], AF.Ln), (s_ac, 1))
            Wt("vector", s_ac)
            Eo("vector", lambda e: e.tensor_scalar(negld[:, :], ldrow[:, :], -1.0, None, ALU.mult), (s_dv, 1))
            ldp = psF[:, 1024:1040]
            Wt("tensor", s_dv)
            Eo("tensor", lambda e, o=ldp: e.matmul(o, onesf[:, :], ldrow[:, :], start=True, stop=True), (s_pe, 1))
            Wt("vector", s_pe)
            Eo("vector", lambda e, i=ldp: e.tensor_copy(ldb[:, :], i), (s_dv, 1))
            if "xnt" in tapnames:
                barrier(("sync",))
                Eo("sync", lambda e: e.dma_start(tapt["xnt"][:, :, :], xnT[:, :, :]), (s_sy, 16))
                Eo("sync", lambda e: e.dma_start(tapt["mask"][:, :], maskT[:, :]), (s_sy, 16))
                Eo("sync", lambda e: e.dma_start(tapt["ld"][:, 0:16], ldrow[:, :]), (s_sy, 16))
                Eo("sync", lambda e: e.dma_start(tapt["ld"][:, 16:32], pdrow[:, :]), (s_sy, 16))
                if "lgt" in tapnames:
                    Eo("sync", lambda e: e.dma_start(tapt["lgt"][:, :], dbglg[:, :]), (s_sy, 16))

        def P2(b, p, cv, costab, sinsig):
            QT = cv.b16(128, T)
            KT = cv.b16(128, T)
            Vb = cv.b16(128, NTT * 130)
            V2b = cv.b16(128, NTT * 130)
            Pbuf = cv.b16(128, 4 * 1024)
            undA = cv.b16(65, T)
            undB = cv.b16(65, T)
            expD = cv.b16(128, 4 * 1024)
            wq_b = cv.b16(128, 8 * 128)
            wk_b = cv.b16(128, 8 * 128)
            wv_b = cv.b16(128, 8 * 128)
            wg_b = cv.b16(128, 8 * 128)
            grawS = cv.b16(128, 1024)
            ropef = cv.f32(128, 128)
            ropeg = cv.f32(128, 128)
            ropeb = cv.b16(128, 2 * 128)
            qn2 = cv.f32(128, 128)
            biasm = cv.f32(128, 2 * 256)
            ffac = cv.f32(128, 8)
            gbufq = cv.f32(65, 2 * 512)
            crow = cv.f32(1, 8)
            negcB = cv.f32(128, 1)
            jM = cv.f32(128, 128)
            s1b = cv.f32(64, 32)
            s2b = cv.f32(64, 2)
            s12 = cv.f32(64, 4)
            jn = cv.f32(128, 1024)
            wmap = {"wq": wq_b, "wk": wk_b, "wv": wv_b, "wg": wg_b}

            barrier()
            for nm in ("wq", "wk", "wv", "wg"):
                Eo("gpsimd", lambda e, nm=nm, p=p, w=wmap[nm]: e.dma_start(
                    w.rearrange("p (c o) -> p c o", c=8),
                    wT[nm][:, 128 * p:128 * p + 128].rearrange("(c q) o -> q c o", q=128)), (s_gp, 16))
            Eo("gpsimd", lambda e, Vb=Vb: e.memset(Vb, 1.0), (s_gc, 1))
            Wt("tensor", s_gp)
            Wt("tensor", s_gc)
            pjfree = {0: 0, 1: 0}
            for tens, wb, brn in (("q", wq_b, "bq"), ("k", wk_b, "bk"), ("v", wv_b, "bv")):
                pend = None
                for tt in range(NTT + 1):
                    if tt < NTT:
                        pj = psF[:, 3072 + 512 * (tt % 2):3072 + 512 * (tt % 2) + 128]
                        Wt("tensor", s_dv, pjfree[tt % 2])
                        for c in range(8):
                            Eo("tensor", lambda e, c=c, tt=tt, wb=wb, o=pj: e.matmul(
                                o, xnT[:, c, 128 * tt:128 * tt + 128],
                                wb.rearrange("p (c o) -> p c o", c=8)[:, c, :],
                                start=(c == 0), stop=False))
                        Eo("tensor", lambda e, o=pj, brn=brn, p=p: e.matmul(
                            o, onesb[0:1, :], brows[brn][:, 128 * p:128 * p + 128],
                            start=False, stop=True), (s_pe, 1))
                        Wt("vector", s_pe)
                        if tens == "v":
                            Eo("vector", lambda e, tt=tt, i=pj, Vb=Vb: e.tensor_scalar(
                                Vb[:, 130 * tt:130 * tt + 64], i[:, 0:64], 0.0, None, ALU.add))
                            Eo("vector", lambda e, tt=tt, i=pj, Vb=Vb: e.tensor_scalar(
                                Vb[:, 130 * tt + 65:130 * tt + 129], i[:, 64:128], 0.0, None, ALU.add), (s_dv, 1))
                            pjfree[tt % 2] = s_dv.n
                        else:
                            pj4 = pj.rearrange("p (h a b) -> p h a b", h=2, b=2)
                            rf4 = ropef.rearrange("p (h a b) -> p h a b", h=2, b=2)
                            rg4 = ropeg.rearrange("p (h a b) -> p h a b", h=2, b=2)
                            cos4 = rep(costab[:, tt, :].rearrange("p (a b) -> p a b", b=2), 2)
                            sin_e = rep(sinsig[:, tt, :].rearrange("p (a b) -> p a b", b=2)[:, :, 0:1], 2)
                            sin_o = rep(sinsig[:, tt, :].rearrange("p (a b) -> p a b", b=2)[:, :, 1:2], 2)
                            Eo("vector", lambda e, pj4=pj4, cos4=cos4, rf4=rf4: e.tensor_tensor(
                                rf4, pj4, cos4, ALU.mult))
                            Eo("vector", lambda e, pj4=pj4, sin_e=sin_e, rg4=rg4: e.tensor_tensor(
                                rg4[:, :, :, 0:1], pj4[:, :, :, 1:2], sin_e, ALU.mult))
                            Eo("vector", lambda e, pj4=pj4, sin_o=sin_o, rg4=rg4: e.tensor_tensor(
                                rg4[:, :, :, 1:2], pj4[:, :, :, 0:1], sin_o, ALU.mult), (s_dv, 1))
                            pjfree[tt % 2] = s_dv.n
                            rsl = tt % 2
                            rb2 = ropeb.rearrange("p (s n) -> p s n", s=2)
                            spacer()
                            Eo("vector", lambda e, rsl=rsl, rb2=rb2, ropef=ropef, ropeg=ropeg: e.tensor_add(
                                rb2[:, rsl, :], ropef, ropeg), (s_dv, 1))
                            spacer()
                            qk = 0 if tens == "q" else 1
                            for h in range(2):
                                Eo("vector", lambda e, rsl=rsl, h=h, qk=qk, tt=tt, rb2=rb2, jM=jM, qn2=qn2: e.scalar_tensor_tensor(
                                    jM[:, 0:64], rb2[:, rsl, 64 * h:64 * h + 64], 1.0,
                                    rb2[:, rsl, 64 * h:64 * h + 64], ALU.bypass, ALU.mult,
                                    accum_out=qn2[:, 64 * qk + 32 * h + tt:64 * qk + 32 * h + tt + 1]), (s_dv, 1))
                    if tens != "v" and pend is not None:
                        ptt, prsl, pdv = pend
                        ptr = psH[:, 1024 * (ptt % 2):1024 * (ptt % 2) + 128]
                        rb2b = ropeb.rearrange("p (s n) -> p s n", s=2)
                        Wt("tensor", s_dv, pdv)
                        Eo("tensor", lambda e, prsl=prsl, o=ptr, rb2b=rb2b: e.transpose(
                            o, rb2b[:, prsl, :], ident[:, :]), (s_pe, 1))
                        Wt("vector", s_pe)
                        dst = QT if tens == "q" else KT
                        Eo("vector", lambda e, ptt=ptt, i=ptr, dst=dst: e.tensor_copy(
                            dst[:, 128 * ptt:128 * ptt + 128], i), (s_dv, 1))
                    if tt < NTT and tens != "v":
                        pend = (tt, tt % 2, s_dv.n)
                    else:
                        pend = pend if tt < NTT else None
            gspf = {0: 0, 1: 0}
            for sp_ in range(8):
                gps = psF[:, 2048 + 512 * (sp_ % 2):2048 + 512 * (sp_ % 2) + 512]
                Wt("tensor", s_dv, s_dv.n)
                for c in range(8):
                    Eo("tensor", lambda e, c=c, sp_=sp_, o=gps, wg_b=wg_b: e.matmul(
                        o, wg_b.rearrange("p (c o) -> p c o", c=8)[:, c, :],
                        xnT[:, c, 512 * sp_:512 * sp_ + 512], start=(c == 0), stop=(c == 7)),
                       *([(s_pe, 1)] if c == 7 else []))
                Wt("vector", s_pe)
                Wt("vector", s_gp, gspf[sp_ % 2])
                Eo("vector", lambda e, sp_=sp_, i=gps, p=p, grawS=grawS: e.tensor_scalar(
                    grawS.rearrange("p (s n) -> p s n", s=2)[:, sp_ % 2, :], i, bgcols[:, p:p + 1], None, ALU.add),
                   (s_dv, 1))
                Wt("gpsimd", s_dv)
                Eo("gpsimd", lambda e, b=b, p=p, sp_=sp_, grawS=grawS: e.dma_start(
                    gatespill[b, p, :, 512 * sp_:512 * sp_ + 512],
                    grawS.rearrange("p (s n) -> p s n", s=2)[:, sp_ % 2, :]), (s_gp, 16))
                gspf[sp_ % 2] = s_gp.n
            if stage < 2:
                return
            for h in range(2):
                Eo("scalar", lambda e, h=h, p=p, ffac=ffac: e.activation(
                    ffac.rearrange("p (h a) -> p h a", h=2)[:, h, 0:1], kmod4[:, 0:1], AF.Exp,
                    scale=ldb[:, 2 * p + h:2 * p + h + 1]), (s_ac, 1))
            Wt("vector", s_ac)
            for tt in range(NTT):
                for h in range(2):
                    Eo("vector", lambda e, tt=tt, h=h, Vb=Vb, V2b=V2b, ffac=ffac: e.tensor_scalar(
                        V2b[:, 130 * tt + 65 * h:130 * tt + 65 * h + 65],
                        Vb[:, 130 * tt + 65 * h:130 * tt + 65 * h + 65],
                        ffac.rearrange("p (h a) -> p h a", h=2)[:, h, 0:1], None, ALU.mult),
                       (s_dv, 1))
            tp1 = psF[:, 1024:1152]
            Wt("tensor", s_dv)
            Eo("tensor", lambda e, o=tp1, qn2=qn2: e.transpose(o, qn2, identf[:, :]), (s_pe, 1))
            Wt("vector", s_pe)
            Eo("vector", lambda e, i=tp1, jM=jM: e.tensor_reduce(jM[:, 0:1], i, AX.X, ALU.max), (s_dv, 1))
            tp2 = psF[:, 1536:1664]
            Wt("tensor", s_dv)
            Eo("tensor", lambda e, o=tp2, jM=jM: e.transpose(o[0:1, :], jM[:, 0:1], identf[:, :]), (s_pe, 1))
            Wt("vector", s_pe)
            Eo("vector", lambda e, i=tp2, crow=crow: e.tensor_reduce(
                crow[:, 0:4], i[0:1, :].rearrange("p (a b) -> p a b", a=4), AX.X, ALU.max), (s_dv, 1))
            spacer()
            Eo("vector", lambda e, crow=crow: e.tensor_reduce(crow[:, 4:5], crow[:, 0:2], AX.X, ALU.max))
            Eo("vector", lambda e, crow=crow: e.tensor_reduce(crow[:, 5:6], crow[:, 2:4], AX.X, ALU.max))
            spacer()
            Eo("vector", lambda e, crow=crow: e.tensor_mul(crow[:, 6:7], crow[:, 4:5], crow[:, 5:6]), (s_dv, 1))
            Wt("scalar", s_dv)
            Eo("scalar", lambda e, crow=crow: e.activation(crow[:, 7:8], crow[:, 6:7], AF.Ln))
            Eo("scalar", lambda e, crow=crow: e.activation(crow[:, 0:1], crow[:, 7:8], AF.Exp,
                                                           scale=0.5, bias=consts[0:1, 1:2]), (s_ac, 1))
            cb = psF[:, 1024:1025]
            Wt("tensor", s_ac)
            Eo("tensor", lambda e, o=cb, crow=crow: e.matmul(o, onesf[:, :], crow[:, 0:1], start=True, stop=True), (s_pe, 1))
            Wt("vector", s_pe)
            Eo("vector", lambda e, i=cb, negcB=negcB: e.tensor_scalar(negcB, i, -1.0, None, ALU.mult), (s_dv, 1))
            for h in range(2):
                Eo("vector", lambda e, h=h, p=p, jM=jM: e.tensor_scalar(
                    jM[:, 64 * h:64 * h + 32], giota[:, :], ldb[:, 2 * p + h:2 * p + h + 1], None, ALU.mult))
            for h in range(2):
                Eo("vector", lambda e, h=h, p=p, jM=jM: e.tensor_scalar(
                    jM[:, 64 * h + 32:64 * h + 40], siota[:, :], ldb[:, 2 * p + h:2 * p + h + 1], None, ALU.mult))
            spacer()
            for h in range(2):
                Eo("vector", lambda e, h=h, jM=jM, negcB=negcB: e.tensor_scalar(
                    jM[:, 64 * h + 40:64 * h + 48], jM[:, 64 * h + 32:64 * h + 40], negcB[:, 0:1], None, ALU.subtract))
            spacer()
            for h in range(2):
                bm = biasm.rearrange("p (h g s) -> p h g s", h=2, g=32)[:, h, :, :]
                m1 = rep(jM[:, 64 * h:64 * h + 32], 8, pos=2)
                m2 = rep(jM[:, 64 * h + 40:64 * h + 48], 32, pos=1)
                Eo("vector", lambda e, bm=bm, m1=m1, m2=m2: e.tensor_tensor(bm, m1, m2, ALU.subtract), (s_dv, 1))
            Wt("scalar", s_dv)
            for h in range(2):
                Eo("scalar", lambda e, h=h, p=p, expD=expD: e.activation(
                    expD.rearrange("p (x a b) -> p x a b", x=2, a=4)[:, h, :, :],
                    DpatT[:, :, :], AF.Exp, scale=ldb[:, 2 * p + h:2 * p + h + 1]), (s_ac, 1))
            for h in range(2):
                Eo("scalar", lambda e, h=h, p=p, jn=jn: e.activation(
                    jn[0:1, 512 * h:512 * h + 512], fiota[:, :], AF.Exp,
                    scale=negld[:, 2 * p + h:2 * p + h + 1]), (s_ac, 1))
            for h in range(2):
                gp_ = psF[:, 1024 + 512 * h:1024 + 512 * h + 512]
                Wt("tensor", s_ac)
                Eo("tensor", lambda e, h=h, o=gp_, jn=jn: e.matmul(
                    o[0:65, :], onesf[:, 0:65], jn[0:1, 512 * h:512 * h + 512],
                    start=True, stop=True), (s_pe, 1))
                Wt("vector", s_pe)
                Eo("vector", lambda e, h=h, i=gp_, gbufq=gbufq: e.tensor_copy(
                    gbufq.rearrange("p (h n) -> p h n", h=2)[:, h, :], i[0:65, :]), (s_dv, 1))
            S_R = [psF[:, 0:1024], psF[:, 1024:2048]]
            accs = [psF[0:65, 2048:3072], psF[0:65, 3072:4096]]
            Pb2 = Pbuf.rearrange("p (s n) -> p s n", s=4)
            bm2 = biasm.rearrange("p (h n) -> p h n", h=2)
            eD2 = expD.rearrange("p (x n) -> p x n", x=2)
            gb2 = gbufq.rearrange("p (h n) -> p h n", h=2)
            fin_dv = s_dv.n
            NGQ = 32
            Wt("tensor", s_dv)
            Wt("tensor", s_ac)

            def emit_scores(s, g):
                r = g % 2
                for h, base in ((0, 0), (1, 64)):
                    Eo("tensor", lambda e, g=g, h=h, base=base, s=s, r=r, QT=QT, KT=KT: e.matmul(
                        S_R[r][:, 512 * h:512 * h + 512],
                        KT[base:base + 64, 128 * g:128 * g + 128],
                        QT[base:base + 64, SP2 * s:SP2 * s + SP2],
                        start=True, stop=True),
                       *([(s_pe, 1)] if h == 1 else []))

            exp_done = {}
            av_done = {}
            scores_done = {}
            for s in range(NSP2):
                Wt("tensor", s_ac, exp_done.get((s - 1, NGQ - 2), 0))
                emit_scores(s, 0)
                scores_done[(s, 0)] = s_pe.n
                for g in range(NGQ):
                    r = g % 2
                    cls = "past" if g < 4 * s else ("diag" if g <= 4 * s + 3 else "fut")
                    Wt("scalar", s_pe, scores_done[(s, g)])
                    Wt("scalar", s_pe, av_done.get(g - 4 if g >= 4 else NGQ + g - 4, 0))
                    if cls != "fut":
                        Eo("scalar", lambda e, r=r, g=g, Pb2=Pb2, negcB=negcB: e.activation(
                            Pb2[:, g % 4, :], S_R[r], AF.Exp, bias=negcB[:, 0:1], scale=0.125), (s_ac, 1))
                    else:
                        for h in range(2):
                            Eo("scalar", lambda e, r=r, g=g, h=h, s=s, Pb2=Pb2, bm2=bm2: e.activation(
                                Pb2[:, g % 4, 512 * h:512 * h + 512], S_R[r][:, 512 * h:512 * h + 512],
                                AF.Exp, bias=bm2[:, h, 8 * g + s:8 * g + s + 1], scale=0.125), (s_ac, 1))
                    exp_done[(s, g)] = s_ac.n
                    if g + 1 < NGQ:
                        Wt("tensor", s_ac, exp_done.get((s, g - 1), exp_done.get((s - 1, NGQ - 1), 0)))
                        emit_scores(s, g + 1)
                        scores_done[(s, g + 1)] = s_pe.n
                    dv_gate = None
                    if cls == "diag":
                        d = g - 4 * s
                        Wt("vector", s_ac, exp_done[(s, g)])
                        for h in range(2):
                            Eo("vector", lambda e, g=g, h=h, d=d, Pb2=Pb2, eD2=eD2: e.tensor_tensor(
                                Pb2[:, g % 4, 512 * h:512 * h + 512], Pb2[:, g % 4, 512 * h:512 * h + 512],
                                eD2[:, h, 512 * d:512 * d + 512], ALU.mult), (s_dv, 1))
                        dv_gate = s_dv.n
                    if dv_gate is not None:
                        Wt("tensor", s_dv, dv_gate)
                    else:
                        Wt("tensor", s_ac, exp_done[(s, g)])
                    if g == 0:
                        Wt("tensor", s_dv, fin_dv)
                    Vsrc = V2b if cls == "fut" else Vb
                    ro = 512 if cls == "fut" else 0
                    if cls == "fut":
                        stf, spf = (g == 4 * s + 4), (g == NGQ - 1)
                    else:
                        stf, spf = (g == 0), (g == 4 * s + 3)
                    for h in range(2):
                        acc = accs[h]
                        Eo("tensor", lambda e, g=g, h=h, acc=acc, ro=ro, Vsrc=Vsrc, s=s, stf=stf, spf=spf, Pb2=Pb2: e.matmul(
                            acc[:, ro:ro + 512],
                            Vsrc[:, 130 * g + 65 * h:130 * g + 65 * h + 65],
                            Pb2[:, g % 4, 512 * h:512 * h + 512],
                            start=stf, stop=spf),
                           *([(s_pe, 1)] if h == 1 else []))
                    av_done[g] = s_pe.n
                Wt("vector", s_pe)
                for h in range(2):
                    acc = accs[h]
                    und = undA if h == 0 else undB
                    if s <= NSP2 - 2:
                        Eo("vector", lambda e, acc=acc, h=h, gb2=gb2, jn=jn: e.tensor_tensor(
                            jn[0:65, 0:512], gb2[:, h, :], acc[:, 512:1024], ALU.mult), (s_dv, 1))
                        spacer()
                        Eo("vector", lambda e, s=s, acc=acc, und=und, jn=jn: e.tensor_tensor(
                            und[:, SP2 * s:SP2 * s + SP2], jn[0:65, 0:512], acc[:, 0:512], ALU.add), (s_dv, 1))
                    else:
                        Eo("vector", lambda e, s=s, acc=acc, und=und: e.tensor_scalar(
                            und[:, SP2 * s:SP2 * s + SP2], acc[:, 0:512], 0.0, None, ALU.add), (s_dv, 1))
                fin_dv = s_dv.n
            Wt("scalar", s_dv)
            for h in range(2):
                und = undA if h == 0 else undB
                Eo("scalar", lambda e, und=und: e.activation(und[64:65, :], und[64:65, :], AF.Ln))
            for h in range(2):
                und = undA if h == 0 else undB
                Eo("scalar", lambda e, und=und: e.activation(und[64:65, :], und[64:65, :], AF.Exp, scale=-1.0),
                   (s_ac, 1))
            Wt("tensor", s_ac)
            for s in range(NSP2):
                for h in range(2):
                    und = undA if h == 0 else undB
                    rp = psF[0:64, 3072 + 512 * (h % 2):3072 + 512 * (h % 2) + 512]
                    Wt("tensor", s_dv, s_dv.n)
                    Eo("tensor", lambda e, s=s, und=und, o=rp: e.matmul(
                        o, onesb[64:65, 0:64], und[64:65, SP2 * s:SP2 * s + SP2], start=True, stop=True), (s_pe, 1))
                    Wt("vector", s_pe)
                    Eo("vector", lambda e, s=s, h=h, und=und, i=rp, s1b=s1b: e.scalar_tensor_tensor(
                        und[0:64, SP2 * s:SP2 * s + SP2], und[0:64, SP2 * s:SP2 * s + SP2], 1.0, i,
                        ALU.bypass, ALU.mult, accum_out=s1b[:, 8 * h + s:8 * h + s + 1]), (s_dv, 1))
            for h in range(2):
                und = undA if h == 0 else undB
                Eo("vector", lambda e, und=und, h=h, V2b=V2b, s2b=s2b: e.scalar_tensor_tensor(
                    V2b[0:64, 0:4096], und[0:64, :], 1.0, und[0:64, :], ALU.bypass, ALU.mult,
                    accum_out=s2b[:, h:h + 1]), (s_dv, 1))
            Eo("vector", lambda e, s1b=s1b, s12=s12: e.tensor_reduce(s12[:, 0:1], s1b[:, 0:8], AX.X, ALU.add))
            Eo("vector", lambda e, s1b=s1b, s12=s12: e.tensor_reduce(s12[:, 2:3], s1b[:, 8:16], AX.X, ALU.add))
            spacer()
            Eo("vector", lambda e, s2b=s2b, s12=s12: e.tensor_copy(s12[:, 1:2], s2b[:, 0:1]))
            Eo("vector", lambda e, s2b=s2b, s12=s12: e.tensor_copy(s12[:, 3:4], s2b[:, 1:2]), (s_dv, 1))
            stp = psF[:, 1024:1028]
            Wt("tensor", s_dv)
            Eo("tensor", lambda e, o=stp, s12=s12: e.matmul(o[0:1, :], onescol[0:64, :], s12,
                                                            start=True, stop=True), (s_pe, 1))
            Wt("vector", s_pe)
            Eo("vector", lambda e, p=p, i=stp: e.tensor_copy(statsrow[:, 4 * p:4 * p + 4], i[0:1, :]), (s_dv, 1))
            Wt("gpsimd", s_dv)
            Eo("gpsimd", lambda e, b=b, p=p, undA=undA: e.dma_start(attnspill[b, p, 0:64, :], undA[0:64, :]), (s_gp, 16))
            Eo("gpsimd", lambda e, b=b, p=p, undB=undB: e.dma_start(attnspill[b, p, 64:128, :], undB[0:64, :]), (s_gp, 16))
            if "att" in tapnames and b == 0 and p == 0:
                barrier(("sync",))
                Eo("sync", lambda e, gbufq=gbufq: e.dma_start(tapt["gbuf"][0:65, :], gbufq), (s_sy, 16))
                Eo("sync", lambda e, ffac=ffac: e.dma_start(tapt["ffac"][:, :], ffac), (s_sy, 16))
                Eo("sync", lambda e, biasm=biasm: e.dma_start(tapt["biasm"][:, :], biasm), (s_sy, 16))
                Eo("sync", lambda e, V2b=V2b: e.dma_start(tapt["v2"][:, :], V2b), (s_sy, 16))
                barrier(("sync",))
                Eo("sync", lambda e, undA=undA: e.dma_start(tapt["att"][0:64, :], undA[0:64, :]), (s_sy, 16))
                Eo("sync", lambda e, undB=undB: e.dma_start(tapt["att"][64:128, :], undB[0:64, :]), (s_sy, 16))
                Eo("sync", lambda e, QT=QT: e.dma_start(tapt["qkt"][0:128, 0:T], QT[:, :]), (s_sy, 16))
                Eo("sync", lambda e, KT=KT: e.dma_start(tapt["qkt"][0:128, T:2 * T], KT[:, :]), (s_sy, 16))
                Eo("sync", lambda e, Vb=Vb: e.dma_start(tapt["vb"][:, :], Vb[:, :]), (s_sy, 16))
                Eo("sync", lambda e, undA=undA: e.dma_start(tapt["nrm"][0:1, :], undA[64:65, :]), (s_sy, 16))
                Eo("sync", lambda e, crow=crow: e.dma_start(tapt["crow"][:, :], crow[:, :]), (s_sy, 16))

        def P3(b, cv):
            woTb = cv.b16(128, 8 * 1024)
            indm = cv.b16(16, 16 * 128)
            hbuf = cv.b16(128, 32 * TS)
            thb = hbuf[:, 0:T]
            attld = hbuf[:, T:2 * T]
            gld = hbuf[:, 2 * T:3 * T]
            gonP = cv.b16(128, 2 * TS)
            w1b = cv.b16(128, 4 * 8 * 128)
            w2b = cv.b16(128, 4 * 1024)
            outf = cv.f32(128, 2 * E)
            jr = cv.f32(1, 256)
            scrow = cv.f32(1, 256)
            gamr = cv.f32(1, E)
            betr = cv.f32(1, E)
            bobr = cv.f32(1, E)
            sc2 = cv.f32(128, 16)

            barrier()
            Eo("gpsimd", lambda e, woTb=woTb: e.dma_start(
                woTb.rearrange("p (c o) -> p c o", c=8),
                wT["wo"][:, :].rearrange("(c p) o -> p c o", p=128)), (s_gp, 16))
            Eo("gpsimd", lambda e, indm=indm: e.memset(indm, 1.0))
            Eo("gpsimd", lambda e, indm=indm: e.affine_select(
                indm.rearrange("p (a b) -> p a b", a=16), indm.rearrange("p (a b) -> p a b", a=16),
                pattern=[[-1, 16], [0, 128]], compare_op=ALU.is_equal, fill=0.0, base=0,
                channel_multiplier=1), (s_gc, 1))
            Eo("gpsimd", lambda e, gamr=gamr: e.dma_start(gamr, vecs["gamma"][:, :]), (s_gp, 16))
            Eo("gpsimd", lambda e, betr=betr: e.dma_start(betr, vecs["beta"][:, :]), (s_gp, 16))
            Eo("gpsimd", lambda e, bobr=bobr: e.dma_start(bobr, vecs["bo"][:, :]), (s_gp, 16))
            Wt("gpsimd", s_gp)
            Eo("gpsimd", lambda e, bobr=bobr: e.dma_start(bobr, vecs["b2"][:, :], accum_op=ALU.add), (s_gp, 16))
            NEL = float(T * HD)
            Eo("vector", lambda e, jr=jr: e.tensor_scalar(jr[:, 0:64], statsrow[:, :], float(1.0 / NEL),
                                                          None, ALU.mult), (s_dv, 1))
            mu_r = bass.AP(jr.tensor, jr.offset, [list(jr.ap[0]), [2, 16]])
            q2_r = bass.AP(jr.tensor, jr.offset + 1, [list(jr.ap[0]), [2, 16]])
            spacer()
            Eo("vector", lambda e, mu_r=mu_r, jr=jr: e.tensor_tensor(jr[:, 64:80], mu_r, mu_r, ALU.mult))
            spacer()
            Eo("vector", lambda e, q2_r=q2_r, jr=jr: e.tensor_tensor(jr[:, 80:96], q2_r, jr[:, 64:80], ALU.subtract),
               (s_dv, 1))
            Wt("scalar", s_dv)
            Eo("scalar", lambda e, jr=jr: e.activation(jr[:, 96:112], jr[:, 80:96], AF.Ln, bias=consts[0:1, 0:1]))
            Eo("scalar", lambda e, jr=jr: e.activation(jr[:, 112:128], jr[:, 96:112], AF.Exp, scale=-0.5), (s_ac, 1))
            Wt("vector", s_ac)
            Wt("vector", s_gp)
            Wt("tensor", s_gc)
            for p in range(npair):
                rsP = bass.AP(jr.tensor, jr.offset + 112 + 2 * p, [list(jr.ap[0]), [1, 2], [0, 64]])
                muP = bass.AP(jr.tensor, jr.offset + 4 * p, [list(jr.ap[0]), [2, 2], [0, 64]])
                Eo("vector", lambda e, p=p, rsP=rsP, scrow=scrow: e.tensor_tensor(
                    scrow[:, 0:128].rearrange("p (a b) -> p a b", a=2),
                    gamr[:, 128 * p:128 * p + 128].rearrange("p (a b) -> p a b", a=2), rsP, ALU.mult), (s_dv, 1))
                spacer()
                Eo("vector", lambda e, muP=muP, scrow=scrow: e.tensor_tensor(
                    scrow[:, 128:256].rearrange("p (a b) -> p a b", a=2),
                    scrow[:, 0:128].rearrange("p (a b) -> p a b", a=2), muP, ALU.mult))
                spacer()
                Eo("vector", lambda e, p=p, scrow=scrow, betr=betr: e.tensor_sub(
                    scrow[:, 128:256], betr[:, 128 * p:128 * p + 128], scrow[:, 128:256]), (s_dv, 1))
                scp = psF[:, 3072 + 512 * (p % 2):3072 + 512 * (p % 2) + 2]
                Wt("tensor", s_dv)
                Eo("tensor", lambda e, o=scp, scrow=scrow: e.matmul(
                    o[:, 0:1], scrow[0:1, 0:128], onesf[0:1, 0:1], start=True, stop=True))
                Eo("tensor", lambda e, o=scp, scrow=scrow: e.matmul(
                    o[:, 1:2], scrow[0:1, 128:256], onesf[0:1, 0:1], start=True, stop=True), (s_pe, 1))
                Wt("vector", s_pe)
                Eo("vector", lambda e, p=p, i=scp, sc2=sc2: e.tensor_copy(sc2[:, 2 * p:2 * p + 2], i), (s_dv, 1))
            if "sc2" in tapnames and b == 0:
                barrier(("sync",))
                Eo("sync", lambda e, sc2=sc2: e.dma_start(tapt["sc2"][:, :], sc2), (s_sy, 16))
                Eo("sync", lambda e, jr=jr: e.dma_start(tapt["jr"][:, :], jr), (s_sy, 16))
                Eo("sync", lambda e: e.dma_start(tapt["strow"][:, :], statsrow[:, :]), (s_sy, 16))
            for p in range(npair):
                Wt("sync", s_gp, s_gp.n)
                Wt("sync", s_dv, s_dv.n)
                Eo("sync", lambda e, b=b, p=p, attld=attld: e.dma_start(attld, attnspill[b, p, :, :]), (s_sy, 16))
                Eo("sync", lambda e, b=b, p=p, gld=gld: e.dma_start(gld, gatespill[b, p, :, :]), (s_sy, 16))
                Wt("scalar", s_sy)
                Eo("scalar", lambda e, gld=gld, thb=thb: e.activation(thb, gld, AF.Tanh, scale=0.5), (s_ac, 1))
                Wt("vector", s_ac)
                Eo("vector", lambda e, p=p, attld=attld, sc2=sc2: e.tensor_scalar(
                    attld, attld, sc2[:, 2 * p:2 * p + 1], sc2[:, 2 * p + 1:2 * p + 2], ALU.mult, ALU.add))
                Eo("vector", lambda e, thb=thb: e.tensor_scalar(thb, thb, 1.0, 0.5, ALU.add, ALU.mult))
                spacer()
                Eo("vector", lambda e, thb=thb, gld=gld: e.tensor_mul(thb, thb, gld))
                spacer()
                Eo("vector", lambda e, thb=thb, attld=attld, gld=gld: e.tensor_mul(gld, thb, attld), (s_dv, 1))
                Wt("gpsimd", s_dv)
                Eo("gpsimd", lambda e, b=b, p=p, gld=gld: e.dma_start(gonspill[b, p, :, :], gld), (s_gp, 16))
            if "gon0" in tapnames and b == 0:
                barrier(("sync",))
                Eo("sync", lambda e, b=b: e.dma_start(tapt["gon0"][:, :], gonspill[b, 0, :, :]), (s_sy, 16))
                Eo("sync", lambda e, b=b: e.dma_start(tapt["asp0"][:, :], attnspill[b, 0, :, :]), (s_sy, 16))
                Eo("sync", lambda e, b=b: e.dma_start(tapt["gsp0"][:, :], gatespill[b, 0, :, :]), (s_sy, 16))
            NL = NTS * 32
            w1b4 = w1b.rearrange("p (s c o) -> p s c o", s=4, c=8)
            w2b4 = w2b.rearrange("p (s n) -> p s n", s=4)
            w1need = {}
            w1slotfree = {}
            w2need = {}
            w2slotfree = {}

            def issue_w1(L):
                och = L % 32
                slot = L % 4
                Wt("gpsimd", s_pe, w1slotfree.get(slot, 0))
                Eo("gpsimd", lambda e, och=och, slot=slot, w1b4=w1b4: e.dma_start(
                    w1b4[:, slot, :, :],
                    w1T[:, 128 * och:128 * och + 128].rearrange("(c q) o -> q c o", q=128)), (s_gp, 16))
                w1need[L] = s_gp.n

            def issue_w2(L):
                och = L % 32
                slot = L % 4
                Wt("gpsimd", s_pe, w2slotfree.get(slot, 0))
                Eo("gpsimd", lambda e, och=och, slot=slot, w2b4=w2b4: e.dma_start(
                    w2b4[:, slot, :], w2T[128 * och:128 * och + 128, :]), (s_gp, 16))
                w2need[L] = s_gp.n
            for _i in range(4):
                issue_w1(_i)
                issue_w2(_i)
            gfree = {0: 0, 1: 0}
            ofree = {0: 0, 1: 0}
            hpfree = {}
            mbfree = {}
            p3xneed = {}
            p3xfree = {}

            def issue_px(LT):
                slot = LT % 4
                Wt("sync", s_dv, p3xfree.get(slot, 0))
                Eo("sync", lambda e, b=b, LT=LT, slot=slot: e.dma_start(
                    xtile[:, slot, :], xin[b, 128 * LT:128 * LT + 128, :]), (s_sy, 16))
                p3xneed[LT] = s_sy.n
            for _i in range(4):
                issue_px(_i)
            for ts_ in range(NTS):
                for och in range(32):
                    L = ts_ * 32 + och
                    if och % 2 == 0:
                        blk = och // 2
                        mbp = psF[:, 1024 + 512 * (blk % 2):1024 + 512 * (blk % 2) + 512]
                        Wt("tensor", s_dv, mbfree.get(blk % 2, 0))
                        Eo("tensor", lambda e, blk=blk, ts_=ts_, o=mbp, indm=indm: e.matmul(
                            o, indm.rearrange("p (a b) -> p a b", a=16)[:, blk, :],
                            maskT[:, TS * ts_:TS * ts_ + TS], start=True, stop=True), (s_pe, 1))
                    hp = psF[:, 512 * (och % 2):512 * (och % 2) + 512]
                    Wt("tensor", s_gp, w1need[L])
                    Wt("tensor", s_dv, hpfree.get(och % 2, 0))
                    for c in range(8):
                        Eo("tensor", lambda e, c=c, och=och, ts_=ts_, L=L, o=hp, w1b4=w1b4: e.matmul(
                            o, w1b4[:, L % 4, c, :],
                            xnT[:, c, TS * ts_:TS * ts_ + TS], start=(c == 0), stop=(c == 7)),
                           *([(s_pe, 1)] if c == 7 else []))
                    w1slotfree[L % 4] = s_pe.n
                    if L + 4 < NL:
                        issue_w1(L + 4)
                    Wt("vector", s_pe)
                    Eo("vector", lambda e, och=och, i=hp, hbuf=hbuf: e.tensor_scalar(
                        hbuf.rearrange("p (o n) -> p o n", o=32)[:, och, :], i, b1cols[:, och:och + 1], 0.0,
                        ALU.add, ALU.max), (s_dv, 1))
                    hpfree[och % 2] = s_dv.n
                    mbp2 = psF[:, 1024 + 512 * ((och // 2) % 2):1024 + 512 * ((och // 2) % 2) + 512]
                    spacer()
                    Eo("vector", lambda e, och=och, i=mbp2, hbuf=hbuf: e.tensor_tensor(
                        hbuf.rearrange("p (o n) -> p o n", o=32)[:, och, :],
                        hbuf.rearrange("p (o n) -> p o n", o=32)[:, och, :], i, ALU.mult), (s_dv, 1))
                    if och % 2 == 1:
                        mbfree[(och // 2) % 2] = s_dv.n
                hdone = s_dv.n
                Wt("tensor", s_dv, hdone)
                for p in range(npair):
                    gsl = p % 2
                    Wt("gpsimd", s_pe, gfree[gsl])
                    Eo("gpsimd", lambda e, b=b, p=p, ts_=ts_, gsl=gsl, gonP=gonP: e.dma_start(
                        gonP.rearrange("p (s n) -> p s n", s=2)[:, gsl, :],
                        gonspill[b, p, :, TS * ts_:TS * ts_ + TS]), (s_gp, 16))
                    Wt("tensor", s_gp)
                    for tt_ in range(4):
                        for dsp in range(2):
                            slot = psF[:, 512 * (2 * tt_ + dsp):512 * (2 * tt_ + dsp) + 512]
                            Eo("tensor", lambda e, p=p, tt_=tt_, dsp=dsp, o=slot, gsl=gsl, gonP=gonP, woTb=woTb: e.matmul(
                                o, gonP.rearrange("p (s n) -> p s n", s=2)[:, gsl, 128 * tt_:128 * tt_ + 128],
                                woTb.rearrange("p (c o) -> p c o", c=8)[:, p, 512 * dsp:512 * dsp + 512],
                                start=(p == 0), stop=False),
                               *([(s_pe, 1)] if (tt_ == 3 and dsp == 1) else []))
                    gfree[gsl] = s_pe.n
                for och in range(32):
                    L = ts_ * 32 + och
                    wsl = L % 4
                    Wt("tensor", s_gp, w2need[L])
                    for tt_ in range(4):
                        for dsp in range(2):
                            slot = psF[:, 512 * (2 * tt_ + dsp):512 * (2 * tt_ + dsp) + 512]
                            Eo("tensor", lambda e, och=och, tt_=tt_, dsp=dsp, o=slot, wsl=wsl, hbuf=hbuf, w2b4=w2b4: e.matmul(
                                o, hbuf.rearrange("p (o n) -> p o n", o=32)[:, och, 128 * tt_:128 * tt_ + 128],
                                w2b4[:, wsl, 512 * dsp:512 * dsp + 512],
                                start=False, stop=False),
                               *([(s_pe, 1)] if (tt_ == 3 and dsp == 1) else []))
                    w2slotfree[wsl] = s_pe.n
                    if L + 4 < NL:
                        issue_w2(L + 4)
                for tt_ in range(4):
                    for dsp in range(2):
                        slot = psF[:, 512 * (2 * tt_ + dsp):512 * (2 * tt_ + dsp) + 512]
                        Eo("tensor", lambda e, tt_=tt_, dsp=dsp, o=slot, bobr=bobr: e.matmul(
                            o, onesf[:, :], bobr[:, 512 * dsp:512 * dsp + 512],
                            start=False, stop=True),
                           *([(s_pe, 1)] if (tt_ == 3 and dsp == 1) else []))
                ydone = s_pe.n
                for tt_ in range(4):
                    tg = 4 * ts_ + tt_
                    xsl = tg % 4
                    Wt("vector", s_sy, p3xneed[tg])
                    Wt("vector", s_pe, ydone)
                    Wt("vector", s_sy, ofree[tt_ % 2])
                    for dsp in range(2):
                        slot = psF[:, 512 * (2 * tt_ + dsp):512 * (2 * tt_ + dsp) + 512]
                        Eo("vector", lambda e, tt_=tt_, dsp=dsp, xsl=xsl, i=slot, outf=outf: e.tensor_tensor(
                            outf.rearrange("p (s n) -> p s n", s=2)[:, tt_ % 2, 512 * dsp:512 * dsp + 512],
                            xtile[:, xsl, 512 * dsp:512 * dsp + 512], i, ALU.add), (s_dv, 1))
                    p3xfree[xsl] = s_dv.n
                    if tg + 4 < 4 * NTS:
                        issue_px(tg + 4)
                    Wt("sync", s_dv)
                    Eo("sync", lambda e, b=b, tg=tg, tt_=tt_, outf=outf: e.dma_start(
                        out[b, 128 * tg:128 * tg + 128, :],
                        outf.rearrange("p (s n) -> p s n", s=2)[:, tt_ % 2, :]), (s_sy, 16))
                    ofree[tt_ % 2] = s_sy.n
                hpfree = {0: s_dv.n, 1: s_dv.n}
                mbfree = {0: s_dv.n, 1: s_dv.n}

        P0()
        for b in range(nb):
            cv = Carver(AR)
            P1(b, cv)
            cv = Carver(AR)
            costab = cv.f16(128, NTT * 64)
            sinsig = cv.f16(128, NTT * 64)
            tsc = cv.f32(128, 512)
            cv2_base = cv.off
            barrier()
            trig(costab.rearrange("p (a b) -> p a b", a=NTT),
                 sinsig.rearrange("p (a b) -> p a b", a=NTT), tsc)
            if "cost" in tapnames and b == 0:
                barrier(("sync",))
                Eo("sync", lambda e, costab=costab: e.dma_start(tapt["cost"][:, :], costab), (s_sy, 16))
                Eo("sync", lambda e, sinsig=sinsig: e.dma_start(tapt["sins"][:, :], sinsig), (s_sy, 16))
            for p in range(npair):
                cv.off = cv2_base
                P2(b, p, cv, costab.rearrange("p (a b) -> p a b", a=NTT),
                   sinsig.rearrange("p (a b) -> p a b", a=NTT))
            if stage >= 3:
                cv = Carver(AR)
                P3(b, cv)
        barrier(("sync", "gpsimd"))

        block = st.enter_context(nc.Block())

        def mk(engname):
            def run(eng):
                for fn, incs in prog[engname]:
                    ins = fn(eng)
                    for hh, k in incs:
                        ins = ins.then_inc(hh, k)
            return run

        block.tensor(mk("tensor"))
        block.vector(mk("vector"))
        block.scalar(mk("scalar"))
        block.gpsimd(mk("gpsimd"))
        block.sync(mk("sync"))

    return nc


def _stage_inputs(inputs):
    ins = {}
    ins["x"] = np.ascontiguousarray(inputs["L0_attn_x"])
    for nm in ("wq", "wk", "wv", "wg", "wo"):
        ins[nm + "T"] = np.ascontiguousarray(inputs["L0_attn_" + nm].T)
    ins["w1T"] = np.ascontiguousarray(inputs["L0_ffn_w1"].T)
    ins["w2T"] = np.ascontiguousarray(inputs["L0_ffn_w2"].T)
    ins["wdT"] = np.ascontiguousarray(inputs["L0_attn_wd"].T)
    ins["rwT"] = np.ascontiguousarray(inputs["L0_ffn_router_w"].T)
    for src, dst in (("L0_attn_bq", "bq"), ("L0_attn_bk", "bk"), ("L0_attn_bv", "bv"),
                     ("L0_attn_bg", "bg"), ("L0_attn_bo", "bo"), ("L0_attn_bd", "bd"),
                     ("L0_ffn_b1", "b1"), ("L0_ffn_b2", "b2"), ("L0_ffn_router_b", "rb"),
                     ("L0_attn_gamma", "gamma"), ("L0_attn_beta", "beta")):
        ins[dst] = np.ascontiguousarray(inputs[src]).reshape(1, -1)
    return ins


def kernel(**inputs):
    ins = _stage_inputs(inputs)
    x = ins.pop("x")
    nc = build()
    in_maps = []
    for i in range(8):
        m = dict(ins)
        m["x"] = np.ascontiguousarray(x[2 * i:2 * i + 2])
        in_maps.append(m)
    res = run_bass_kernel_spmd(nc, in_maps, core_ids=list(range(8)))
    return np.concatenate([r["out"] for r in res.results], axis=0)
